# version 15
# baseline (speedup 1.0000x reference)
import numpy as np
from contextlib import ExitStack

import concourse.bass as bass
import concourse.mybir as mybir
from concourse.bass_utils import run_bass_kernel_spmd

F32 = mybir.dt.float32
BF16 = mybir.dt.bfloat16
FP8 = mybir.dt.float8e4
U32 = mybir.dt.uint32

AF = mybir.ActivationFunctionType
OP = mybir.AluOpType
AX = mybir.AxisListType

NCORES = 8
T = 1024
TT = 8
DM = 2048
FC = 16
DI = 8192
KS = 8
KI = 1024
NH = 16
S = 4096
RANKS = 4
EPS = 1e-5
SCALE_QK = float(1.0 / np.sqrt(128.0))


class Prog:

    ENGS = ("tensor", "vector", "scalar", "gpsimd", "sync")

    def __init__(self):
        self.sem_count = {}
        self.sem_handles = {}
        self.last_w = {}
        self.readers = {}
        self.clock = {e: {} for e in self.ENGS}
        self.stream = {e: [] for e in self.ENGS}
        self.sem_names = set()

    def sem(self, name):
        self.sem_names.add(name)
        self.sem_count.setdefault(name, 0)
        return name

    def _need(self, eng, dep):
        s, v, snap = dep
        waits = []
        c = self.clock[eng]
        if c.get(s, 0) < v:
            waits.append((s, v))
            c[s] = v
            if snap:
                for ss, vv in snap.items():
                    if c.get(ss, 0) < vv:
                        c[ss] = vv
        return waits

    def _gather(self, eng, reads, writes, war):
        waits = []
        for k in reads:
            d = self.last_w.get(k)
            if d is not None:
                waits += self._need(eng, d)
        for k in tuple(writes) + tuple(war):
            d = self.last_w.get(k)
            if d is not None:
                waits += self._need(eng, d)
            for d in self.readers.get(k, ()):
                waits += self._need(eng, d)
        return waits

    def op(self, eng, fn, reads=(), writes=(), war=(), sem=None, inc=1):
        return self.group(eng, [fn], reads, writes, war, sem, inc)

    def group(self, eng, fns, reads=(), writes=(), war=(), sem=None, inc=1):
        waits = self._gather(eng, reads, writes, war)
        if eng == "tensor" and not waits:
            waits = [("__nudge", 0)]
            self.sem_names.add("__nudge")
            self.sem_count.setdefault("__nudge", 0)
        s = sem if sem is not None else eng
        self.sem_names.add(s)
        self.sem_count[s] = self.sem_count.get(s, 0) + inc
        val = self.sem_count[s]
        for i, fn in enumerate(fns):
            self.stream[eng].append(
                (waits if i == 0 else [], fn,
                 s if i == len(fns) - 1 else None, inc))
        dep = (s, val, dict(self.clock[eng]))
        for k in writes:
            self.last_w[k] = dep
            self.readers[k] = []
        for k in reads:
            self.readers.setdefault(k, []).append(dep)
        return dep

    def barrier(self):
        counts = dict(self.sem_count)
        for e in self.ENGS:
            waits = []
            for s, v in counts.items():
                if v > 0 and self.clock[e].get(s, 0) < v:
                    waits.append((s, v))
                    self.clock[e][s] = v
            if waits:
                self.stream[e].append((waits, None, None, 0))

    def emit(self, nc, es):
        for s in sorted(self.sem_names):
            self.sem_handles[s] = es.enter_context(nc.semaphore(s))
        block = es.enter_context(nc.Block())
        prog = self

        def runner(ename):
            def run(engine):
                for waits, fn, s, inc in prog.stream[ename]:
                    for ws, wv in waits:
                        engine.wait_ge(prog.sem_handles[ws], wv)
                    if fn is not None:
                        instr = fn(engine)
                        if s is not None:
                            instr.then_inc(prog.sem_handles[s], inc)
            return run

        block.tensor(runner("tensor"))
        block.vector(runner("vector"))
        block.scalar(runner("scalar"))
        block.gpsimd(runner("gpsimd"))
        block.sync(runner("sync"))


def build():
    nc = bass.Bass("TRN2", target_bir_lowering=False)
    p = Prog()

    x_in = nc.dram_tensor("x", [T, DM], F32, kind="ExternalInput")
    gw_my = nc.dram_tensor("gw_my", [KI, DM], F32, kind="ExternalInput")
    uw_my = nc.dram_tensor("uw_my", [KI, DM], F32, kind="ExternalInput")
    dw_my = nc.dram_tensor("dw_my", [DM, KI], F32, kind="ExternalInput")
    wq = nc.dram_tensor("wq", [16, 128, DM], F32, kind="ExternalInput")
    wk = nc.dram_tensor("wk", [16, 128, DM], F32, kind="ExternalInput")
    wv = nc.dram_tensor("wv", [16, 128, DM], F32, kind="ExternalInput")
    wo = nc.dram_tensor("wo", [16, 128, DM], F32, kind="ExternalInput")
    bqc = nc.dram_tensor("bqc", [128, 16], F32, kind="ExternalInput")
    bkc = nc.dram_tensor("bkc", [128, 16], F32, kind="ExternalInput")
    bvc = nc.dram_tensor("bvc", [128, 16], F32, kind="ExternalInput")
    bo = nc.dram_tensor("bo", [1, DM], F32, kind="ExternalInput")
    y_out = nc.dram_tensor("y", [T, DM], F32, kind="ExternalOutput")

    scr_my = {}
    scr_all = {}
    for icol in range(2):
        scr_my[icol] = nc.dram_tensor(
            f"scr_my_{icol}", [32, 128, 512], BF16)
        scr_all[icol] = nc.dram_tensor(
            f"scr_all_{icol}", [KS * 32, 128, 512], BF16,
            addr_space="Shared")
    scr_d_my = nc.dram_tensor("scr_d_my", [32, 128, 512], BF16)
    scr_d = nc.dram_tensor("scr_d", [KS * 32, 128, 512], BF16,
                           addr_space="Shared")
    ot_scr = nc.dram_tensor("ot_scr", [NH, 128, T], BF16)
    kt_bounce = nc.dram_tensor("kt_bounce", [DM, T], BF16)
    v_bounce = nc.dram_tensor("v_bounce", [T, DM], BF16)
    kt_all = nc.dram_tensor("kt_all", [RANKS * DM, T], BF16)
    v_all = nc.dram_tensor("v_all", [RANKS * T, DM], BF16)

    es = ExitStack()

    def sb(name, shape, dt):
        return es.enter_context(nc.sbuf_tensor(name, shape, dt))

    x_res = sb("x_res", [128, TT * DM], F32)
    bo_bf = sb("bo_bf", [1, DM], BF16)
    ones_bf = sb("ones_bf", [128, 128], BF16)
    onesf = sb("onesf", [1, 512], F32)
    bq_s = sb("bq_s", [128, 16], F32)
    bk_s = sb("bk_s", [128, 16], F32)
    bv_s = sb("bv_s", [128, 16], F32)
    xnb = sb("xnb", [128, DM], F32)
    stat = sb("stat", [128, 16], F32)
    wq8f = sb("wq8f", [128, 4096], FP8)
    wq8bf = wq8f
    wdq4 = sb("wdq4", [128, 4096], BF16)
    q8 = wq8f[:, 0:2048]
    q8b = q8
    wdqb = wdq4[:, 0:2048]
    mab = [sb(f"mab{i}", [128, 128], F32) for i in range(2)]
    sc2b = [sb(f"sc2b{i}", [128, 128], F32) for i in range(2)]
    rc2b = [sb(f"rc2b{i}", [128, 128], F32) for i in range(2)]

    arF = sb("arF", [128, 8192], F32)
    arB = sb("arB", [128, 40960], BF16)

    def cF(off, n):
        return arF[:, off:off + n]

    def cB(off, n):
        return arB[:, off:off + n]

    gate_s = cF(0, 4096)
    wfb = [cF(4096 + i * 2048, 2048) for i in range(2)]
    xdqT = cB(0, FC * T)
    hdqT = cB(16384, KI * TT)
    pnlA = cB(24576, 8192)
    pnlB = cB(32768, 8192)
    asm = cB(32768, 8192)
    wcol = [cF(0 + i * 2048, 2048) for i in range(2)]
    qt_buf = cB(16384, NH * T)
    wcolb = [cB(32768 + i * 2048, 2048) for i in range(2)]
    kstage = cB(36864, 1024)
    vstage = cB(37888, 1024)
    vtr = cB(38912, 1024)
    kbuf = [cB(0 + i * S, S) for i in range(2)]
    vbuf = [cB(8192 + i * S, S) for i in range(2)]
    pt = [cB(32768 + i * 2048, 2048) for i in range(3)]
    otstage = cB(38912, 1024)
    serow = cF(4096, 1024)
    serec = cF(5120, 1024)
    serbs = cF(6144, 1024)
    otrow = [cB(16384 + i * 1024, 1024) for i in range(2)]
    wrowb = [cB(18432 + i * 512, 512) for i in range(2)]
    wrow = [cF(7168 + i * 512, 512) for i in range(2)]

    psA = es.enter_context(nc.psum_tensor("psA", [128, 2048], F32))
    psO = es.enter_context(nc.psum_tensor("psO", [128, 1024], F32))
    psS = es.enter_context(nc.psum_tensor("psS", [128, 1024], F32))

    def psh(j):
        if j < 4:
            return psA[:, j * 512:(j + 1) * 512]
        if j < 6:
            return psO[:, (j - 4) * 512:(j - 3) * 512]
        return psS[:, (j - 6) * 512:(j - 5) * 512]

    DQ_W = p.sem("dq_w")
    DQ_T = p.sem("dq_t")
    DQ_P = p.sem("dq_p")
    DQ_S = p.sem("dq_s")
    DQ_X = p.sem("dq_x")
    DQ_K = p.sem("dq_k")
    DQ_G = p.sem("dq_g")
    DQ_GA = p.sem("dq_ga")
    DQ_GB = p.sem("dq_gb")
    DQ_GW2 = p.sem("dq_gw2")
    DQ_GK = p.sem("dq_gk")
    DQ_GV = p.sem("dq_gv")
    DQ_GC = p.sem("dq_gc")
    DQ_GO = p.sem("dq_go")
    DQ_GR = p.sem("dq_gr")
    DQ_GKB = p.sem("dq_gkb")
    DQ_GVB = p.sem("dq_gvb")
    CC = p.sem("cc")

    p.op("vector", lambda e: e.memset(ones_bf[:, :], 1.0), writes=["ones_bf"])
    p.op("vector", lambda e: e.memset(onesf[:, :], 1.0), writes=["onesrow"])
    p.op("gpsimd", lambda e: e.dma_start(out=bo_bf[:, :], in_=bo[:, :]),
         writes=["borow"], sem=DQ_G, inc=16)
    p.op("scalar", lambda e: e.dma_start(out=bq_s[:, :], in_=bqc[:, :]),
         writes=["bq_s"], sem=DQ_X, inc=16)
    p.op("scalar", lambda e: e.dma_start(out=bk_s[:, :], in_=bkc[:, :]),
         writes=["bk_s"], sem=DQ_X, inc=16)
    p.op("scalar", lambda e: e.dma_start(out=bv_s[:, :], in_=bvc[:, :]),
         writes=["bv_s"], sem=DQ_X, inc=16)
    p.op("scalar",
         lambda e: e.dma_start(
             out=x_res[:, :].rearrange("p (t f) -> p t f", f=DM),
             in_=x_in[:, :].rearrange("(t p) f -> t p f", p=128)),
         writes=[("x", tb) for tb in range(TT)], sem=DQ_X, inc=16)

    def quant_tile(src_ap, src_keys, W, i, dst_q8, dst_q8b, dst_dq, keys):
        nb = W // 32
        ma = mab[i][:, 0:nb]
        sc2 = sc2b[i][:, 0:nb]
        rc2 = rc2b[i][:, 0:nb]
        kma, ksc, krc = ("ma", i), ("sc", i), ("rc", i)
        src3 = src_ap.rearrange("p (n b) -> p n b", b=32)
        p.op("vector",
             lambda e: e.tensor_reduce(ma.unsqueeze(2), src3, axis=AX.X,
                                       op=OP.max, apply_absolute_value=True),
             reads=src_keys, writes=[kma])
        p.op("vector",
             lambda e: e.tensor_scalar(sc2, ma, 2.0 / 448.0, 2e-30,
                                       OP.mult, OP.max),
             reads=[kma], writes=[ksc])
        p.op("vector", lambda e: e.reciprocal(rc2, sc2),
             reads=[ksc], writes=[krc])
        rcb = rc2.unsqueeze(2).broadcast_to((128, nb, 32))
        p.op("vector",
             lambda e: e.tensor_tensor(
                 dst_q8.rearrange("p (n b) -> p n b", b=32), src3, rcb,
                 op=OP.mult),
             reads=src_keys + [krc], writes=[keys[0]])
        p.op("vector",
             lambda e: e.tensor_copy(dst_q8b.bitcast(U32), dst_q8.bitcast(U32)),
             reads=[keys[0]], writes=[keys[1]])
        scb = sc2.unsqueeze(2).broadcast_to((128, nb, 32))
        p.op("vector",
             lambda e: e.tensor_tensor(
                 dst_dq.rearrange("p (n b) -> p n b", b=32),
                 dst_q8b.rearrange("p (n b) -> p n b", b=32), scb,
                 op=OP.mult),
             reads=[keys[1], ksc], writes=[keys[2]])

    def layernorm_tile(tb):
        xt = x_res[:, tb * DM:(tb + 1) * DM]
        xk = [("x", tb)]
        p.op("scalar",
             lambda e: e.activation(xnb[:, :], xt, AF.Square,
                                    accum_out=stat[:, 1:2]),
             reads=xk, writes=["xnb", "st1"])
        p.op("vector",
             lambda e: e.tensor_reduce(
                 stat[:, 0:1].unsqueeze(2),
                 xt.rearrange("p (a f) -> p a f", a=1), axis=AX.X, op=OP.add),
             reads=xk, writes=["st0"])
        p.op("vector",
             lambda e: e.tensor_scalar(stat[:, 2:3], stat[:, 0:1],
                                       -1.0 / DM, None, OP.mult),
             reads=["st0"], writes=["st2"])
        p.op("vector",
             lambda e: e.tensor_scalar(stat[:, 3:4], stat[:, 1:2],
                                       1.0 / DM, None, OP.mult),
             reads=["st1"], writes=["st3"])
        p.op("vector",
             lambda e: e.scalar_tensor_tensor(stat[:, 5:6], stat[:, 2:3],
                                              stat[:, 2:3], stat[:, 3:4],
                                              OP.mult, OP.subtract),
             reads=["st2", "st3"], writes=["st5"])
        p.op("vector",
             lambda e: e.tensor_scalar(stat[:, 6:7], stat[:, 5:6],
                                       -1.0, EPS, OP.mult, OP.add),
             reads=["st5"], writes=["st6"])
        p.op("scalar", lambda e: e.sqrt(stat[:, 7:8], stat[:, 6:7]),
             reads=["st6"], writes=["st7"])
        p.op("vector", lambda e: e.reciprocal(stat[:, 8:9], stat[:, 7:8]),
             reads=["st7"], writes=["st8"])
        p.op("vector",
             lambda e: e.tensor_tensor(stat[:, 9:10], stat[:, 8:9],
                                       stat[:, 8:9], op=OP.mult),
             reads=["st8"], writes=["st9"])
        p.op("vector",
             lambda e: e.tensor_tensor(stat[:, 10:11], stat[:, 9:10],
                                       stat[:, 6:7], op=OP.mult),
             reads=["st9", "st6"], writes=["st10"])
        p.op("vector",
             lambda e: e.tensor_scalar(stat[:, 11:12], stat[:, 10:11],
                                       -0.5, 1.5, OP.mult, OP.add),
             reads=["st10"], writes=["st11"])
        p.op("vector",
             lambda e: e.tensor_tensor(stat[:, 4:5], stat[:, 8:9],
                                       stat[:, 11:12], op=OP.mult),
             reads=["st8", "st11"], writes=["st4"])
        p.op("vector",
             lambda e: e.tensor_scalar(xnb[:, :], xt, stat[:, 2:3],
                                       stat[:, 4:5], OP.add, OP.mult),
             reads=xk + ["st2", "st4"], writes=["xnb"])

    xdqT_3d = xdqT.rearrange("p (fc t) -> p fc t", t=T)
    hdqT_3d = hdqT.rearrange("p (ic t) -> p ic t", t=T)
    asm_3d = asm.rearrange("p (c q) -> p c q", q=512)

    def ln_and_quant_x_gen():
        for tb in range(TT):
            layernorm_tile(tb)
            quant_tile(xnb[:, :], ["xnb"], DM, tb % 2, q8[:, :], q8b[:, :],
                       wdqb[:, :], [("q8",), ("q8b",), ("wdq",)])
            p.op("sync",
                 lambda e, tb=tb: e.dma_start(
                     out=xdqT_3d[:, :, tb * 128:(tb + 1) * 128],
                     in_=wdqb[:, :].rearrange("p (c q) -> p c q", q=128),
                     transpose=True),
                 reads=[("wdq",)], writes=[("xdqT", tb)], sem=DQ_T, inc=16)
            yield

    def ln_and_quant_x(bg=None):
        for _ in ln_and_quant_x_gen():
            pass

    def quant_my_weights(bg=None):
        allg = [[0, 1, 2, 3, 4, 5, 6, 7]]
        wf4 = arF[:, 4096:8192]
        for icol in range(2):
            for pi, src_t in ((0, gw_my), (1, uw_my)):
                for ab in range(2):
                    a0 = icol * 4 + 2 * ab
                    kw, kw2 = ("wf", 0), ("wf", 1)
                    p.op("scalar",
                         lambda e, src_t=src_t, a0=a0: e.dma_start(
                             out=wfb[0],
                             in_=src_t[a0 * 128:(a0 + 1) * 128, :]),
                         writes=[kw], war=[kw], sem=DQ_X, inc=16)
                    p.op("scalar",
                         lambda e, src_t=src_t, a0=a0: e.dma_start(
                             out=wfb[1],
                             in_=src_t[(a0 + 1) * 128:(a0 + 2) * 128, :]),
                         writes=[kw2], war=[kw2], sem=DQ_S, inc=16)
                    quant_tile(wf4, [kw, kw2], 4096, ab % 2, wq8f[:, :],
                               wq8f[:, :], wdq4[:, :],
                               [("q8",), ("q8b",), ("wdq",)])
                    for j in range(2):
                        aa = (a0 + j) % 4
                        wsl = wdq4[:, j * 2048:(j + 1) * 2048]
                        p.op("sync",
                             lambda e, aa=aa, wsl=wsl: e.dma_start(
                                 out=asm_3d[:, :, aa * 128:(aa + 1) * 128],
                                 in_=wsl.rearrange("p (c q) -> p c q", q=128),
                                 transpose=True),
                             reads=[("wdq",)], writes=[("asm", aa)],
                             sem=DQ_T, inc=16)
                p.op("scalar",
                     lambda e, pi=pi, icol=icol: e.dma_start(
                         out=scr_my[icol][pi * 16:(pi + 1) * 16].rearrange(
                             "c p q -> p c q"),
                         in_=asm_3d),
                     reads=[("asm", 0), ("asm", 1), ("asm", 2), ("asm", 3)],
                     writes=[("scrmy", icol, pi)], sem=DQ_S, inc=16)
            p.op("gpsimd",
                 lambda e, icol=icol: e.collective_compute(
                     "AllGather", OP.bypass, replica_groups=allg,
                     ins=[scr_my[icol].ap().opt()],
                     outs=[scr_all[icol].ap().opt()]),
                 reads=[("scrmy", icol, 0), ("scrmy", icol, 1)],
                 writes=[("scr", icol)], sem=CC, inc=1)
            if bg is not None and icol == 0:
                for _ in range(4):
                    next(bg, None)
        for ob in range(8):
            oc0 = 2 * ob
            kw, kw2 = ("wf", 0), ("wf", 1)
            p.op("scalar",
                 lambda e, oc0=oc0: e.dma_start(
                     out=wfb[0][:, 0:KI],
                     in_=dw_my[oc0 * 128:(oc0 + 1) * 128, :]),
                 writes=[kw], war=[kw], sem=DQ_X, inc=16)
            p.op("scalar",
                 lambda e, oc0=oc0: e.dma_start(
                     out=wfb[0][:, KI:2 * KI],
                     in_=dw_my[(oc0 + 1) * 128:(oc0 + 2) * 128, :]),
                 writes=[kw2], war=[kw2], sem=DQ_S, inc=16)
            quant_tile(wfb[0], [kw, kw2], 2048, ob % 2, q8[:, :], q8b[:, :],
                       wdqb[:, :], [("q8",), ("q8b",), ("wdq",)])
            for j in range(2):
                oc = oc0 + j
                wsl = wdqb[:, j * KI:(j + 1) * KI]
                p.op("sync",
                     lambda e, oc=oc, wsl=wsl: e.dma_start(
                         out=asm_3d[:, 0:8,
                                    (oc % 4) * 128:(oc % 4 + 1) * 128],
                         in_=wsl.rearrange("p (c q) -> p c q", q=128),
                         transpose=True),
                     reads=[("wdq",)], writes=[("asm", oc % 4)],
                     sem=DQ_T, inc=16)
            oc = oc0 + 1
            if oc % 4 == 3:
                ocol = oc // 4
                p.op("scalar",
                     lambda e, ocol=ocol: e.dma_start(
                         out=scr_d_my[ocol * 8:(ocol + 1) * 8].rearrange(
                             "c p q -> p c q"),
                         in_=asm_3d[:, 0:8, :]),
                     reads=[("asm", 0), ("asm", 1), ("asm", 2), ("asm", 3)],
                     writes=[("scrmy", "d")], sem=DQ_S, inc=16)
            if bg is not None and ob % 2 == 1:
                next(bg, None)
        p.op("gpsimd",
             lambda e: e.collective_compute(
                 "AllGather", OP.bypass, replica_groups=allg,
                 ins=[scr_d_my.ap().opt()], outs=[scr_d.ap().opt()]),
             reads=[("scrmy", "d")], writes=["scr_d"], sem=CC, inc=1)

    pnlA_3d = pnlA.rearrange("p (c q) -> p c q", q=512)
    pnlB_3d = pnlB.rearrange("p (c q) -> p c q", q=512)

    def ffn_chunk(k, bg=None, store=False):
        def load_gu(pi, icol, dst3d, key):
            row0 = k * 32 + pi * 16
            war = [key] + ([("asm", j) for j in range(4)]
                           if key == ("pnlB",) else [])
            p.op("gpsimd",
                 lambda e, pi=pi, icol=icol, row0=row0: e.dma_start(
                     out=dst3d,
                     in_=scr_all[icol][row0:row0 + 16].rearrange(
                         "c p q -> p c q")),
                 reads=[("scr", icol)], writes=[key], war=war,
                 sem=(DQ_GA if key == ("pnlA",) else DQ_GB), inc=16)

        for icol in range(2):
            load_gu(0, icol, pnlA_3d, ("pnlA",))
            load_gu(1, icol, pnlB_3d, ("pnlB",))
            for tb in range(TT):
                p.group(
                    "tensor",
                    [lambda e, fc=fc, tb=tb: e.matmul(
                        psh(tb), xdqT_3d[:, fc, tb * 128:(tb + 1) * 128],
                        pnlA_3d[:, fc, :],
                        start=(fc == 0), stop=(fc == FC - 1))
                     for fc in range(FC)],
                    reads=[("xdqT", tb), ("pnlA",)],
                    writes=[("ps", tb)], war=[("ps", tb)])
                p.op("scalar",
                     lambda e, tb=tb: e.activation(
                         gate_s[:, tb * 512:(tb + 1) * 512], psh(tb),
                         AF.Silu),
                     reads=[("ps", tb)], writes=[("gs", tb)])
            for tb in range(TT):
                p.group(
                    "tensor",
                    [lambda e, fc=fc, tb=tb: e.matmul(
                        psh(tb), xdqT_3d[:, fc, tb * 128:(tb + 1) * 128],
                        pnlB_3d[:, fc, :],
                        start=(fc == 0), stop=(fc == FC - 1))
                     for fc in range(FC)],
                    reads=[("xdqT", tb), ("pnlB",)],
                    writes=[("ps", tb)], war=[("ps", tb)])
                p.op("vector",
                     lambda e, tb=tb: e.tensor_tensor(
                         gate_s[:, tb * 512:(tb + 1) * 512], psh(tb),
                         gate_s[:, tb * 512:(tb + 1) * 512], op=OP.mult),
                     reads=[("ps", tb), ("gs", tb)], writes=[("gs", tb)])
                if tb % 4 == 3:
                    tbb = tb // 4
                    gsl = gate_s[:, tbb * 2048:(tbb + 1) * 2048]
                    quant_tile(gsl,
                               [("gs", 4 * tbb), ("gs", 4 * tbb + 1),
                                ("gs", 4 * tbb + 2), ("gs", 4 * tbb + 3)],
                               2048, tbb % 2, q8[:, :], q8b[:, :],
                               wdqb[:, :], [("q8",), ("q8b",), ("wdq",)])
                    for j in range(4):
                        tbx = 4 * tbb + j
                        p.op("sync",
                             lambda e, j=j, tbx=tbx, icol=icol: e.dma_start(
                                 out=hdqT_3d[:, icol * 4:(icol + 1) * 4,
                                             tbx * 128:(tbx + 1) * 128],
                                 in_=wdqb[:, j * 512:(j + 1) * 512]
                                 .rearrange("p (c q) -> p c q", q=128),
                                 transpose=True),
                             reads=[("wdq",)],
                             writes=[("hdqT", icol, tbb)],
                             sem=DQ_T, inc=16)
        for half, (pnl3d, pk) in enumerate(((pnlA_3d, ("pnlA",)),
                                            (pnlB_3d, ("pnlB",)))):
            p.op("gpsimd",
                 lambda e, half=half, pnl3d=pnl3d: e.dma_start(
                     out=pnl3d,
                     in_=scr_d[k * 32 + half * 16:
                               k * 32 + (half + 1) * 16].rearrange(
                         "c p q -> p c q")),
                 reads=["scr_d"], writes=[pk],
                 war=[pk] + [("asm", j) for j in range(4)],
                 sem=(DQ_GA if pk == ("pnlA",) else DQ_GB), inc=16)
            for ocol in (2 * half, 2 * half + 1):
                for tb in range(TT):
                    hkeys = [("hdqT", 0, tb // 4), ("hdqT", 1, tb // 4)]
                    p.group(
                        "tensor",
                        [lambda e, ic=ic, tb=tb, ocol=ocol, pnl3d=pnl3d:
                         e.matmul(
                            psh(tb), hdqT_3d[:, ic, tb * 128:(tb + 1) * 128],
                            pnl3d[:, (ocol % 2) * 8 + ic, :],
                            start=(ic == 0), stop=(ic == 7))
                         for ic in range(8)],
                        reads=hkeys + [pk],
                        writes=[("ps", tb)], war=[("ps", tb)])
                    sl = slice(tb * DM + ocol * 512,
                               tb * DM + (ocol + 1) * 512)
                    p.op("vector",
                         lambda e, tb=tb, sl=sl: e.scalar_tensor_tensor(
                             x_res[:, sl], psh(tb), 0.5, x_res[:, sl],
                             OP.mult, OP.add),
                         reads=[("ps", tb), ("x", tb)], writes=[("x", tb)])
                    if ocol == 3:
                        if bg is not None:
                            next(bg, None)
                        if store:
                            y3 = y_out.rearrange("(q t) f -> q t f", t=TT)
                            p.op("scalar",
                                 lambda e, tb=tb, y3=y3: e.dma_start(
                                     out=y3[:, tb, :],
                                     in_=x_res[:, tb * DM:(tb + 1) * DM]),
                                 reads=[("x", tb)], writes=[("y", tb)],
                                 sem=DQ_X, inc=16)

    xnT_3d = xdqT_3d

    def ln2_gen():
        for tb in range(TT):
            layernorm_tile(tb)
            p.op("vector", lambda e: e.tensor_copy(wdqb[:, :], xnb[:, :]),
                 reads=["xnb"], writes=[("wdq",)])
            p.op("sync",
                 lambda e, tb=tb: e.dma_start(
                     out=xdqT_3d[:, :, tb * 128:(tb + 1) * 128],
                     in_=wdqb[:, :].rearrange("p (c q) -> p c q", q=128),
                     transpose=True),
                 reads=[("wdq",)], writes=[("xdqT", tb)], sem=DQ_T, inc=16)
            yield

    def attention(bg=None):
        xnT_keys = [("xdqT", tb) for tb in range(TT)]

        for wsrc, bias, bkey, name in ((wk, bk_s, "bk_s", "k"),
                                       (wv, bv_s, "bv_s", "v"),
                                       (wq, bq_s, "bq_s", "q")):
            for oc in range(16):
                ci = oc % 2
                p.op("scalar",
                     lambda e, wsrc=wsrc, oc=oc, ci=ci: e.dma_start(
                         out=wcol[ci], in_=wsrc[oc]),
                     writes=[("wcol", ci)], war=[("wcol", ci)],
                     sem=(DQ_GC if ci else DQ_G), inc=16)
                p.op("scalar",
                     lambda e, ci=ci: e.copy(wcolb[ci], wcol[ci]),
                     reads=[("wcol", ci)], writes=[("wcolb", ci)])
                for tg in range(2):
                    jps = (oc * 2 + tg) % 8
                    mm = [lambda e, fc=fc, tg=tg, ci=ci, jps=jps: e.matmul(
                        psh(jps),
                        wcolb[ci][:, fc * 128:(fc + 1) * 128],
                        xnT_3d[:, fc, tg * 512:(tg + 1) * 512],
                        start=(fc == 0), stop=(fc == FC - 1))
                        for fc in range(FC)]
                    p.group("tensor", mm,
                            reads=xnT_keys + [("wcolb", ci)],
                            writes=[("ps", jps)], war=[("ps", jps)])
                    if name == "q":
                        p.op("vector",
                             lambda e, oc=oc, tg=tg, jps=jps, bias=bias:
                             e.tensor_scalar(
                                 qt_buf[:, oc * T + tg * 512:
                                        oc * T + (tg + 1) * 512],
                                 psh(jps), bias[:, oc:oc + 1],
                                 None, OP.add),
                             reads=[("ps", jps), bkey], writes=[("qt", oc)])
                    elif name == "k":
                        p.op("vector",
                             lambda e, tg=tg, jps=jps, bias=bias, oc=oc:
                             e.tensor_scalar(
                                 kstage[:, tg * 512:(tg + 1) * 512],
                                 psh(jps), bias[:, oc:oc + 1],
                                 None, OP.add),
                             reads=[("ps", jps), bkey], writes=[("kst", tg)])
                    else:
                        p.op("vector",
                             lambda e, tg=tg, jps=jps, bias=bias, oc=oc:
                             e.tensor_scalar(
                                 vstage[:, tg * 512:(tg + 1) * 512],
                                 psh(jps), bias[:, oc:oc + 1],
                                 None, OP.add),
                             reads=[("ps", jps), bkey], writes=[("vst", tg)])
                if name == "k":
                    p.op("scalar",
                         lambda e, oc=oc: e.dma_start(
                             out=kt_bounce[oc * 128:(oc + 1) * 128, :],
                             in_=kstage),
                         reads=[("kst", 0), ("kst", 1)], writes=[("ktb", oc)],
                         sem=DQ_GKB, inc=16)
                elif name == "v":
                    p.op("sync",
                         lambda e: e.dma_start(
                             out=vtr.rearrange("p (j q) -> p j q", q=128),
                             in_=vstage.rearrange("p (j q) -> p j q", q=128),
                             transpose=True),
                         reads=[("vst", 0), ("vst", 1)], writes=["vtr"],
                         war=["vtr"], sem=DQ_T, inc=16)
                    p.op("scalar",
                         lambda e, oc=oc: e.dma_start(
                             out=v_bounce[:, oc * 128:(oc + 1) * 128]
                             .rearrange("(j p) q -> p j q", p=128),
                             in_=vtr.rearrange("p (j q) -> p j q", q=128)),
                         reads=["vtr"], writes=[("vb", oc)], sem=DQ_GVB, inc=16)
            groups = [[0, 1, 2, 3], [4, 5, 6, 7]]
            if name == "k":
                p.op("gpsimd",
                     lambda e: e.collective_compute(
                         "AllGather", OP.bypass, replica_groups=groups,
                         ins=[kt_bounce.ap().opt()], outs=[kt_all.ap().opt()]),
                     reads=[("ktb", oc) for oc in range(16)],
                     writes=["kt_all"], sem=CC, inc=1)
            elif name == "v":
                p.op("gpsimd",
                     lambda e: e.collective_compute(
                         "AllGather", OP.bypass, replica_groups=groups,
                         ins=[v_bounce.ap().opt()], outs=[v_all.ap().opt()]),
                     reads=[("vb", oc) for oc in range(16)],
                     writes=["v_all"], sem=CC, inc=1)

        p.barrier()

        def load_head(h, i):
            for r in range(RANKS):
                p.op("gpsimd",
                     lambda e, h=h, r=r, i=i: e.dma_start(
                         out=kbuf[i][:, r * T:(r + 1) * T],
                         in_=kt_all[r * DM + h * 128:
                                    r * DM + (h + 1) * 128, :]),
                     reads=["kt_all"], writes=[("kbuf", i)],
                     war=[("kbuf", i)], sem=(DQ_GK if i else DQ_G), inc=16)
                p.op("gpsimd",
                     lambda e, h=h, r=r, i=i: e.dma_start(
                         out=vbuf[i].rearrange(
                             "p (c q) -> p c q", q=128)[:, r * 8:(r + 1) * 8,
                                                        :],
                         in_=v_all[r * T:(r + 1) * T,
                                   h * 128:(h + 1) * 128].rearrange(
                             "(c p) q -> p c q", p=128)),
                     reads=["v_all"], writes=[("vbuf", i)],
                     war=[("vbuf", i)], sem=(DQ_GV if i else DQ_GVB), inc=16)

        load_head(0, 0)
        NCH = S // 128
        NB2 = NCH // 2
        psA_keys = [("ps", j) for j in range(4)]
        psO_keys = [("ps", 4), ("ps", 5)]
        psS_keys = [("ps", 6), ("ps", 7)]
        for h in range(NH):
            i = h % 2
            if h + 1 < NH:
                load_head(h + 1, (h + 1) % 2)

            def av_batch(bc):
                c0, c1 = 2 * bc, 2 * bc + 1
                mm = []
                for cj, c in enumerate((c0, c1)):
                    for tg in range(2):
                        mm.append(lambda e, c=c, cj=cj, tg=tg, i=i, bc=bc:
                                  e.matmul(
                                      psO[:, tg * 512:(tg + 1) * 512],
                                      vbuf[i][:, c * 128:(c + 1) * 128],
                                      pt[bc % 3][:, cj * 1024 + tg * 512:
                                                 cj * 1024 + (tg + 1) * 512],
                                      start=(c == 0), stop=(c == NCH - 1)))
                p.group("tensor", mm,
                        reads=[("pt", bc % 3), ("vbuf", i)],
                        writes=psO_keys if c1 == NCH - 1 else [],
                        war=psO_keys if bc == 0 else [])

            def sum_batch(bc):
                c0, c1 = 2 * bc, 2 * bc + 1
                mm = []
                for cj, c in enumerate((c0, c1)):
                    for tg in range(2):
                        mm.append(lambda e, c=c, cj=cj, tg=tg, bc=bc:
                                  e.matmul(
                                      psS[0:1, tg * 512:(tg + 1) * 512],
                                      ones_bf[:, 0:1],
                                      pt[bc % 3][:, cj * 1024 + tg * 512:
                                                 cj * 1024 + (tg + 1) * 512],
                                      start=(c == 0), stop=(c == NCH - 1)))
                p.group("tensor", mm,
                        reads=[("pt", bc % 3)],
                        writes=psS_keys if c1 == NCH - 1 else [],
                        war=psS_keys if bc == 0 else [])

            for bc in range(NB2):
                c0, c1 = 2 * bc, 2 * bc + 1
                mm = []
                for cj, c in enumerate((c0, c1)):
                    for tg in range(2):
                        mm.append(lambda e, c=c, tg=tg, cj=cj, i=i, h=h:
                                  e.matmul(
                                      psA[:, cj * 1024 + tg * 512:
                                          cj * 1024 + (tg + 1) * 512],
                                      kbuf[i][:, c * 128:(c + 1) * 128],
                                      qt_buf[:, h * T + tg * 512:
                                             h * T + (tg + 1) * 512],
                                      start=True, stop=True))
                p.group("tensor", mm, reads=[("kbuf", i), ("qt", h)],
                        writes=psA_keys, war=psA_keys)
                p.op("scalar",
                     lambda e, bc=bc: e.activation(
                         pt[bc % 3][:, :], psA[:, :], AF.Exp,
                         scale=SCALE_QK),
                     reads=psA_keys, writes=[("pt", bc % 3)])
                if bc >= 2:
                    sum_batch(bc - 2)
                if bc >= 1:
                    av_batch(bc - 1)
            av_batch(NB2 - 1)
            sum_batch(NB2 - 2)
            sum_batch(NB2 - 1)
            p.op("scalar", lambda e: e.copy(otstage, psO[:, :]),
                 reads=psO_keys, writes=["otb"], war=["otb"])
            p.op("vector", lambda e: e.tensor_copy(serow[0:1, :],
                                                   psS[0:1, :]),
                 reads=psS_keys, writes=["serow"], war=["serow"])
            p.op("vector", lambda e: e.reciprocal(serec[0:1, :],
                                                  serow[0:1, :]),
                 reads=["serow"], writes=["serec"])
            p.group("tensor",
                    [lambda e, tg=tg: e.matmul(
                        psS[:, tg * 512:(tg + 1) * 512],
                        onesf[0:1, 0:128],
                        serec[0:1, tg * 512:(tg + 1) * 512],
                        start=True, stop=True) for tg in range(2)],
                    reads=["serec", "onesrow"], writes=psS_keys,
                    war=psS_keys)
            p.op("scalar", lambda e: e.copy(serbs, psS[:, :]),
                 reads=psS_keys, writes=["serbs"])
            p.op("vector",
                 lambda e: e.tensor_tensor(otstage, otstage, serbs,
                                           op=OP.mult),
                 reads=["otb", "serbs"], writes=["otb"])
            p.op("scalar",
                 lambda e, h=h: e.dma_start(out=ot_scr[h], in_=otstage),
                 reads=["otb"], writes=[("ot", h)], sem=DQ_S, inc=16)

        p.barrier()

        for ocol in range(4):
            for dc in range(16):
                di = dc % 2
                p.op("gpsimd",
                     lambda e, dc=dc, di=di: e.dma_start(
                         out=otrow[di], in_=ot_scr[dc]),
                     reads=[("ot", dc)], writes=[("otrow", di)],
                     war=[("otrow", di)], sem=(DQ_GO if di else DQ_G), inc=16)
                p.op("gpsimd",
                     lambda e, dc=dc, ocol=ocol, di=di: e.dma_start(
                         out=wrow[di],
                         in_=wo[dc][:, ocol * 512:(ocol + 1) * 512]),
                     writes=[("wrow", di)], war=[("wrow", di)],
                     sem=(DQ_GR if di else DQ_GKB), inc=16)
                p.op("vector",
                     lambda e, di=di: e.tensor_copy(wrowb[di], wrow[di]),
                     reads=[("wrow", di)], writes=[("wrowb", di)])
                for tb in range(TT):
                    p.group(
                        "tensor",
                        [lambda e, dc=dc, tb=tb, di=di: e.matmul(
                            psh(tb),
                            otrow[di][:, tb * 128:(tb + 1) * 128],
                            wrowb[di], start=(dc == 0), stop=False)],
                        reads=[("otrow", di), ("wrowb", di)],
                        writes=[],
                        war=[("ps", tb)] if dc == 0 else [],
                    )
            for tb in range(TT):
                p.group(
                    "tensor",
                    [lambda e, tb=tb, ocol=ocol: e.matmul(
                        psh(tb), ones_bf[0:1, 0:128],
                        bo_bf[0:1, ocol * 512:(ocol + 1) * 512],
                        start=False, stop=True)],
                    reads=["ones_bf", "borow"],
                    writes=[("ps", tb)],
                )
            for tb in range(TT):
                sl = slice(tb * DM + ocol * 512, tb * DM + (ocol + 1) * 512)
                p.op("vector",
                     lambda e, tb=tb, sl=sl: e.scalar_tensor_tensor(
                         x_res[:, sl], psh(tb), 1.0, x_res[:, sl],
                         OP.mult, OP.add),
                     reads=[("ps", tb), ("x", tb)], writes=[("x", tb)])
                if bg is not None and ocol == 3:
                    next(bg, None)

    _bg = ln_and_quant_x_gen()
    quant_my_weights(bg=_bg)
    for _ in _bg:
        pass
    _ln2 = ln2_gen()
    for k in range(KS):
        ffn_chunk(k, bg=_ln2 if k == KS - 1 else None)
    for _ in _ln2:
        pass
    p.barrier()
    _bg2 = ln_and_quant_x_gen()
    attention(bg=_bg2)
    for _ in _bg2:
        pass
    p.barrier()
    for k in range(KS):
        ffn_chunk(k, store=(k == KS - 1))
    p.barrier()

    p.emit(nc, es)
    es.close()
    return nc


def kernel(L0_attn_x, L0_attn_Wq, L0_attn_bq, L0_attn_Wk, L0_attn_bk,
           L0_attn_Wv, L0_attn_bv, L0_attn_Wout, L0_attn_bout,
           L0_ffn_gate_w, L0_ffn_up_w, L0_ffn_down_w):
    x = np.ascontiguousarray(
        np.asarray(L0_attn_x, dtype=np.float32).reshape(2 * S, DM))
    gw = np.asarray(L0_ffn_gate_w, np.float32)
    uw = np.asarray(L0_ffn_up_w, np.float32)
    dwf = np.asarray(L0_ffn_down_w, np.float32)

    def blk(W):
        W = np.asarray(W, dtype=np.float32)
        return np.ascontiguousarray(
            W.reshape(16, 128, 16, 128).transpose(0, 3, 2, 1)
        ).reshape(16, 128, DM)

    def col(b):
        return np.ascontiguousarray(
            np.asarray(b, dtype=np.float32).reshape(16, 128).T)

    common = {
        "wq": blk(L0_attn_Wq), "wk": blk(L0_attn_Wk), "wv": blk(L0_attn_Wv),
        "wo": np.ascontiguousarray(
            np.asarray(L0_attn_Wout, np.float32).reshape(DM, 16, 128)
            .transpose(1, 2, 0)),
        "bqc": col(L0_attn_bq), "bkc": col(L0_attn_bk),
        "bvc": col(L0_attn_bv),
        "bo": np.ascontiguousarray(
            np.asarray(L0_attn_bout, np.float32).reshape(1, DM)),
    }
    in_maps = []
    for c in range(NCORES):
        m = dict(common)
        m["x"] = np.ascontiguousarray(x[c * T:(c + 1) * T])
        m["gw_my"] = np.ascontiguousarray(gw[c * KI:(c + 1) * KI])
        m["uw_my"] = np.ascontiguousarray(uw[c * KI:(c + 1) * KI])
        m["dw_my"] = np.ascontiguousarray(dwf[:, c * KI:(c + 1) * KI])
        in_maps.append(m)

    nc = build()
    res = run_bass_kernel_spmd(nc, in_maps, core_ids=list(range(NCORES)))
    y = np.concatenate([res.results[c]["y"] for c in range(NCORES)], axis=0)
    return y.reshape(2, S, DM)


# version 16 (final)
# speedup vs baseline: 1.0568x; 1.0568x over previous
import numpy as np
from contextlib import ExitStack

import concourse.bass as bass
import concourse.mybir as mybir
from concourse.bass_utils import run_bass_kernel_spmd

F32 = mybir.dt.float32
BF16 = mybir.dt.bfloat16
FP8 = mybir.dt.float8e4
U32 = mybir.dt.uint32

AF = mybir.ActivationFunctionType
OP = mybir.AluOpType
AX = mybir.AxisListType

NCORES = 8
T = 1024
TT = 8
DM = 2048
FC = 16
DI = 8192
KS = 8
KI = 1024
NH = 16
S = 4096
RANKS = 4
EPS = 1e-5
SCALE_QK = float(1.0 / np.sqrt(128.0))


class Prog:

    ENGS = ("tensor", "vector", "scalar", "gpsimd", "sync")

    def __init__(self):
        self.sem_count = {}
        self.sem_handles = {}
        self.last_w = {}
        self.readers = {}
        self.clock = {e: {} for e in self.ENGS}
        self.stream = {e: [] for e in self.ENGS}
        self.sem_names = set()

    def sem(self, name):
        self.sem_names.add(name)
        self.sem_count.setdefault(name, 0)
        return name

    def _need(self, eng, dep):
        s, v, snap = dep
        waits = []
        c = self.clock[eng]
        if c.get(s, 0) < v:
            waits.append((s, v))
            c[s] = v
            if snap:
                for ss, vv in snap.items():
                    if c.get(ss, 0) < vv:
                        c[ss] = vv
        return waits

    def _gather(self, eng, reads, writes, war):
        waits = []
        for k in reads:
            d = self.last_w.get(k)
            if d is not None:
                waits += self._need(eng, d)
        for k in tuple(writes) + tuple(war):
            d = self.last_w.get(k)
            if d is not None:
                waits += self._need(eng, d)
            for d in self.readers.get(k, ()):
                waits += self._need(eng, d)
        return waits

    def op(self, eng, fn, reads=(), writes=(), war=(), sem=None, inc=1):
        return self.group(eng, [fn], reads, writes, war, sem, inc)

    def group(self, eng, fns, reads=(), writes=(), war=(), sem=None, inc=1):
        waits = self._gather(eng, reads, writes, war)
        s = sem if sem is not None else eng
        self.sem_names.add(s)
        self.sem_count[s] = self.sem_count.get(s, 0) + inc
        val = self.sem_count[s]
        for i, fn in enumerate(fns):
            self.stream[eng].append(
                (waits if i == 0 else [], fn,
                 s if i == len(fns) - 1 else None, inc))
        dep = (s, val, dict(self.clock[eng]))
        for k in writes:
            self.last_w[k] = dep
            self.readers[k] = []
        for k in reads:
            self.readers.setdefault(k, []).append(dep)
        return dep

    def barrier(self):
        counts = dict(self.sem_count)
        for e in self.ENGS:
            waits = []
            for s, v in counts.items():
                if v > 0 and self.clock[e].get(s, 0) < v:
                    waits.append((s, v))
                    self.clock[e][s] = v
            if waits:
                self.stream[e].append((waits, None, None, 0))

    def emit(self, nc, es):
        for s in sorted(self.sem_names):
            self.sem_handles[s] = es.enter_context(nc.semaphore(s))
        block = es.enter_context(nc.Block())
        prog = self

        def runner(ename):
            def run(engine):
                for waits, fn, s, inc in prog.stream[ename]:
                    for ws, wv in waits:
                        engine.wait_ge(prog.sem_handles[ws], wv)
                    if fn is not None:
                        instr = fn(engine)
                        if s is not None:
                            instr.then_inc(prog.sem_handles[s], inc)
            return run

        block.tensor(runner("tensor"))
        block.vector(runner("vector"))
        block.scalar(runner("scalar"))
        block.gpsimd(runner("gpsimd"))
        block.sync(runner("sync"))


def build():
    nc = bass.Bass("TRN2", target_bir_lowering=False)
    p = Prog()

    x_in = nc.dram_tensor("x", [T, DM], F32, kind="ExternalInput")
    gw_my = nc.dram_tensor("gw_my", [KI, DM], F32, kind="ExternalInput")
    uw_my = nc.dram_tensor("uw_my", [KI, DM], F32, kind="ExternalInput")
    dw_my = nc.dram_tensor("dw_my", [DM, KI], F32, kind="ExternalInput")
    wq = nc.dram_tensor("wq", [16, 128, DM], F32, kind="ExternalInput")
    wk = nc.dram_tensor("wk", [16, 128, DM], F32, kind="ExternalInput")
    wv = nc.dram_tensor("wv", [16, 128, DM], F32, kind="ExternalInput")
    wo = nc.dram_tensor("wo", [16, 128, DM], F32, kind="ExternalInput")
    bqc = nc.dram_tensor("bqc", [128, 16], F32, kind="ExternalInput")
    bkc = nc.dram_tensor("bkc", [128, 16], F32, kind="ExternalInput")
    bvc = nc.dram_tensor("bvc", [128, 16], F32, kind="ExternalInput")
    bo = nc.dram_tensor("bo", [1, DM], F32, kind="ExternalInput")
    y_out = nc.dram_tensor("y", [T, DM], F32, kind="ExternalOutput")

    scr_my = {}
    scr_all = {}
    for icol in range(2):
        scr_my[icol] = nc.dram_tensor(
            f"scr_my_{icol}", [32, 128, 512], BF16)
        scr_all[icol] = nc.dram_tensor(
            f"scr_all_{icol}", [KS * 32, 128, 512], BF16,
            addr_space="Shared")
    scr_d_my = nc.dram_tensor("scr_d_my", [32, 128, 512], BF16)
    scr_d = nc.dram_tensor("scr_d", [KS * 32, 128, 512], BF16,
                           addr_space="Shared")
    ot_scr = nc.dram_tensor("ot_scr", [NH, 128, T], BF16)
    kt_bounce = nc.dram_tensor("kt_bounce", [DM, T], BF16)
    v_bounce = nc.dram_tensor("v_bounce", [T, DM], BF16)
    kt_all = nc.dram_tensor("kt_all", [RANKS * DM, T], BF16)
    v_all = nc.dram_tensor("v_all", [RANKS * T, DM], BF16)

    es = ExitStack()

    def sb(name, shape, dt):
        return es.enter_context(nc.sbuf_tensor(name, shape, dt))

    x_res = sb("x_res", [128, TT * DM], F32)
    bo_bf = sb("bo_bf", [1, DM], BF16)
    ones_bf = sb("ones_bf", [128, 128], BF16)
    onesf = sb("onesf", [1, 512], F32)
    bq_s = sb("bq_s", [128, 16], F32)
    bk_s = sb("bk_s", [128, 16], F32)
    bv_s = sb("bv_s", [128, 16], F32)
    xnb = sb("xnb", [128, DM], F32)
    stat = sb("stat", [128, 16], F32)
    wq8f = sb("wq8f", [128, 4096], FP8)
    wq8bf = wq8f
    wdq4 = sb("wdq4", [128, 4096], BF16)
    q8 = wq8f[:, 0:2048]
    q8b = q8
    wdqb = wdq4[:, 0:2048]
    mab = [sb(f"mab{i}", [128, 128], F32) for i in range(2)]
    sc2b = [sb(f"sc2b{i}", [128, 128], F32) for i in range(2)]
    rc2b = [sb(f"rc2b{i}", [128, 128], F32) for i in range(2)]

    arF = sb("arF", [128, 8192], F32)
    arB = sb("arB", [128, 40960], BF16)

    def cF(off, n):
        return arF[:, off:off + n]

    def cB(off, n):
        return arB[:, off:off + n]

    gate_s = cF(0, 4096)
    wfb = [cF(4096 + i * 2048, 2048) for i in range(2)]
    xdqT = cB(0, FC * T)
    hdqT = cB(16384, KI * TT)
    pnlA = cB(24576, 8192)
    pnlB = cB(32768, 8192)
    asm = cB(32768, 8192)
    wcol = [cF(0 + i * 2048, 2048) for i in range(2)]
    qt_buf = cB(16384, NH * T)
    wcolb = [cB(32768 + i * 2048, 2048) for i in range(2)]
    kstage = cB(36864, 1024)
    vstage = cB(37888, 1024)
    vtr = cB(38912, 1024)
    kbuf = [cB(0 + i * S, S) for i in range(2)]
    vbuf = [cB(8192 + i * S, S) for i in range(2)]
    pt = [cB(32768 + i * 2048, 2048) for i in range(3)]
    otstage = cB(38912, 1024)
    serow = cF(4096, 1024)
    serec = cF(5120, 1024)
    serbs = cF(6144, 1024)
    otrow = [cB(16384 + i * 1024, 1024) for i in range(2)]
    wrowb = [cB(18432 + i * 512, 512) for i in range(2)]
    wrow = [cF(7168 + i * 512, 512) for i in range(2)]

    psA = es.enter_context(nc.psum_tensor("psA", [128, 2048], F32))
    psO = es.enter_context(nc.psum_tensor("psO", [128, 1024], F32))
    psS = es.enter_context(nc.psum_tensor("psS", [128, 1024], F32))

    def psh(j):
        if j < 4:
            return psA[:, j * 512:(j + 1) * 512]
        if j < 6:
            return psO[:, (j - 4) * 512:(j - 3) * 512]
        return psS[:, (j - 6) * 512:(j - 5) * 512]

    DQ_W = p.sem("dq_w")
    DQ_T = p.sem("dq_t")
    DQ_P = p.sem("dq_p")
    DQ_S = p.sem("dq_s")
    DQ_X = p.sem("dq_x")
    DQ_K = p.sem("dq_k")
    DQ_G = p.sem("dq_g")
    DQ_GA = p.sem("dq_ga")
    DQ_GB = p.sem("dq_gb")
    DQ_GW2 = p.sem("dq_gw2")
    DQ_GK = p.sem("dq_gk")
    DQ_GV = p.sem("dq_gv")
    DQ_GC = p.sem("dq_gc")
    DQ_GO = p.sem("dq_go")
    DQ_GR = p.sem("dq_gr")
    DQ_GKB = p.sem("dq_gkb")
    DQ_GVB = p.sem("dq_gvb")
    CC = p.sem("cc")

    p.op("vector", lambda e: e.memset(ones_bf[:, :], 1.0), writes=["ones_bf"])
    p.op("vector", lambda e: e.memset(onesf[:, :], 1.0), writes=["onesrow"])
    p.op("gpsimd", lambda e: e.dma_start(out=bo_bf[:, :], in_=bo[:, :]),
         writes=["borow"], sem=DQ_G, inc=16)
    p.op("scalar", lambda e: e.dma_start(out=bq_s[:, :], in_=bqc[:, :]),
         writes=["bq_s"], sem=DQ_X, inc=16)
    p.op("scalar", lambda e: e.dma_start(out=bk_s[:, :], in_=bkc[:, :]),
         writes=["bk_s"], sem=DQ_X, inc=16)
    p.op("scalar", lambda e: e.dma_start(out=bv_s[:, :], in_=bvc[:, :]),
         writes=["bv_s"], sem=DQ_X, inc=16)
    p.op("scalar",
         lambda e: e.dma_start(
             out=x_res[:, :].rearrange("p (t f) -> p t f", f=DM),
             in_=x_in[:, :].rearrange("(t p) f -> t p f", p=128)),
         writes=[("x", tb) for tb in range(TT)], sem=DQ_X, inc=16)

    def quant_tile(src_ap, src_keys, W, i, dst_q8, dst_q8b, dst_dq, keys):
        nb = W // 32
        ma = mab[i][:, 0:nb]
        sc2 = sc2b[i][:, 0:nb]
        rc2 = rc2b[i][:, 0:nb]
        kma, ksc, krc = ("ma", i), ("sc", i), ("rc", i)
        src3 = src_ap.rearrange("p (n b) -> p n b", b=32)
        p.op("vector",
             lambda e: e.tensor_reduce(ma.unsqueeze(2), src3, axis=AX.X,
                                       op=OP.max, apply_absolute_value=True),
             reads=src_keys, writes=[kma])
        p.op("vector",
             lambda e: e.tensor_scalar(sc2, ma, 2.0 / 448.0, 2e-30,
                                       OP.mult, OP.max),
             reads=[kma], writes=[ksc])
        p.op("vector", lambda e: e.reciprocal(rc2, sc2),
             reads=[ksc], writes=[krc])
        rcb = rc2.unsqueeze(2).broadcast_to((128, nb, 32))
        p.op("vector",
             lambda e: e.tensor_tensor(
                 dst_q8.rearrange("p (n b) -> p n b", b=32), src3, rcb,
                 op=OP.mult),
             reads=src_keys + [krc], writes=[keys[0]])
        p.op("vector",
             lambda e: e.tensor_copy(dst_q8b.bitcast(U32), dst_q8.bitcast(U32)),
             reads=[keys[0]], writes=[keys[1]])
        scb = sc2.unsqueeze(2).broadcast_to((128, nb, 32))
        p.op("vector",
             lambda e: e.tensor_tensor(
                 dst_dq.rearrange("p (n b) -> p n b", b=32),
                 dst_q8b.rearrange("p (n b) -> p n b", b=32), scb,
                 op=OP.mult),
             reads=[keys[1], ksc], writes=[keys[2]])

    def layernorm_tile(tb):
        xt = x_res[:, tb * DM:(tb + 1) * DM]
        xk = [("x", tb)]
        p.op("scalar",
             lambda e: e.activation(xnb[:, :], xt, AF.Square,
                                    accum_out=stat[:, 1:2]),
             reads=xk, writes=["xnb", "st1"])
        p.op("vector",
             lambda e: e.tensor_reduce(
                 stat[:, 0:1].unsqueeze(2),
                 xt.rearrange("p (a f) -> p a f", a=1), axis=AX.X, op=OP.add),
             reads=xk, writes=["st0"])
        p.op("vector",
             lambda e: e.tensor_scalar(stat[:, 2:3], stat[:, 0:1],
                                       -1.0 / DM, None, OP.mult),
             reads=["st0"], writes=["st2"])
        p.op("vector",
             lambda e: e.tensor_scalar(stat[:, 3:4], stat[:, 1:2],
                                       1.0 / DM, None, OP.mult),
             reads=["st1"], writes=["st3"])
        p.op("vector",
             lambda e: e.scalar_tensor_tensor(stat[:, 5:6], stat[:, 2:3],
                                              stat[:, 2:3], stat[:, 3:4],
                                              OP.mult, OP.subtract),
             reads=["st2", "st3"], writes=["st5"])
        p.op("vector",
             lambda e: e.tensor_scalar(stat[:, 6:7], stat[:, 5:6],
                                       -1.0, EPS, OP.mult, OP.add),
             reads=["st5"], writes=["st6"])
        p.op("scalar", lambda e: e.sqrt(stat[:, 7:8], stat[:, 6:7]),
             reads=["st6"], writes=["st7"])
        p.op("vector", lambda e: e.reciprocal(stat[:, 8:9], stat[:, 7:8]),
             reads=["st7"], writes=["st8"])
        p.op("vector",
             lambda e: e.tensor_tensor(stat[:, 9:10], stat[:, 8:9],
                                       stat[:, 8:9], op=OP.mult),
             reads=["st8"], writes=["st9"])
        p.op("vector",
             lambda e: e.tensor_tensor(stat[:, 10:11], stat[:, 9:10],
                                       stat[:, 6:7], op=OP.mult),
             reads=["st9", "st6"], writes=["st10"])
        p.op("vector",
             lambda e: e.tensor_scalar(stat[:, 11:12], stat[:, 10:11],
                                       -0.5, 1.5, OP.mult, OP.add),
             reads=["st10"], writes=["st11"])
        p.op("vector",
             lambda e: e.tensor_tensor(stat[:, 4:5], stat[:, 8:9],
                                       stat[:, 11:12], op=OP.mult),
             reads=["st8", "st11"], writes=["st4"])
        p.op("vector",
             lambda e: e.tensor_scalar(xnb[:, :], xt, stat[:, 2:3],
                                       stat[:, 4:5], OP.add, OP.mult),
             reads=xk + ["st2", "st4"], writes=["xnb"])

    xdqT_3d = xdqT.rearrange("p (fc t) -> p fc t", t=T)
    hdqT_3d = hdqT.rearrange("p (ic t) -> p ic t", t=T)
    asm_3d = asm.rearrange("p (c q) -> p c q", q=512)

    def ln_and_quant_x_gen():
        for tb in range(TT):
            layernorm_tile(tb)
            quant_tile(xnb[:, :], ["xnb"], DM, tb % 2, q8[:, :], q8b[:, :],
                       wdqb[:, :], [("q8",), ("q8b",), ("wdq",)])
            p.op("sync",
                 lambda e, tb=tb: e.dma_start(
                     out=xdqT_3d[:, :, tb * 128:(tb + 1) * 128],
                     in_=wdqb[:, :].rearrange("p (c q) -> p c q", q=128),
                     transpose=True),
                 reads=[("wdq",)], writes=[("xdqT", tb)], sem=DQ_T, inc=16)
            yield

    def ln_and_quant_x(bg=None):
        for _ in ln_and_quant_x_gen():
            pass

    def quant_my_weights(bg=None):
        allg = [[0, 1, 2, 3, 4, 5, 6, 7]]
        wf4 = arF[:, 4096:8192]
        for icol in range(2):
            for pi, src_t in ((0, gw_my), (1, uw_my)):
                for ab in range(2):
                    a0 = icol * 4 + 2 * ab
                    kw, kw2 = ("wf", 0), ("wf", 1)
                    p.op("scalar",
                         lambda e, src_t=src_t, a0=a0: e.dma_start(
                             out=wfb[0],
                             in_=src_t[a0 * 128:(a0 + 1) * 128, :]),
                         writes=[kw], war=[kw], sem=DQ_X, inc=16)
                    p.op("scalar",
                         lambda e, src_t=src_t, a0=a0: e.dma_start(
                             out=wfb[1],
                             in_=src_t[(a0 + 1) * 128:(a0 + 2) * 128, :]),
                         writes=[kw2], war=[kw2], sem=DQ_S, inc=16)
                    quant_tile(wf4, [kw, kw2], 4096, ab % 2, wq8f[:, :],
                               wq8f[:, :], wdq4[:, :],
                               [("q8",), ("q8b",), ("wdq",)])
                    for j in range(2):
                        aa = (a0 + j) % 4
                        wsl = wdq4[:, j * 2048:(j + 1) * 2048]
                        p.op("sync",
                             lambda e, aa=aa, wsl=wsl: e.dma_start(
                                 out=asm_3d[:, :, aa * 128:(aa + 1) * 128],
                                 in_=wsl.rearrange("p (c q) -> p c q", q=128),
                                 transpose=True),
                             reads=[("wdq",)], writes=[("asm", aa)],
                             sem=DQ_T, inc=16)
                p.op("scalar",
                     lambda e, pi=pi, icol=icol: e.dma_start(
                         out=scr_my[icol][pi * 16:(pi + 1) * 16].rearrange(
                             "c p q -> p c q"),
                         in_=asm_3d),
                     reads=[("asm", 0), ("asm", 1), ("asm", 2), ("asm", 3)],
                     writes=[("scrmy", icol, pi)], sem=DQ_S, inc=16)
            p.op("gpsimd",
                 lambda e, icol=icol: e.collective_compute(
                     "AllGather", OP.bypass, replica_groups=allg,
                     ins=[scr_my[icol].ap().opt()],
                     outs=[scr_all[icol].ap().opt()]),
                 reads=[("scrmy", icol, 0), ("scrmy", icol, 1)],
                 writes=[("scr", icol)], sem=CC, inc=1)
            if bg is not None and icol == 0:
                for _ in range(4):
                    next(bg, None)
        for ob in range(8):
            oc0 = 2 * ob
            kw, kw2 = ("wf", 0), ("wf", 1)
            p.op("scalar",
                 lambda e, oc0=oc0: e.dma_start(
                     out=wfb[0][:, 0:KI],
                     in_=dw_my[oc0 * 128:(oc0 + 1) * 128, :]),
                 writes=[kw], war=[kw], sem=DQ_X, inc=16)
            p.op("scalar",
                 lambda e, oc0=oc0: e.dma_start(
                     out=wfb[0][:, KI:2 * KI],
                     in_=dw_my[(oc0 + 1) * 128:(oc0 + 2) * 128, :]),
                 writes=[kw2], war=[kw2], sem=DQ_S, inc=16)
            quant_tile(wfb[0], [kw, kw2], 2048, ob % 2, q8[:, :], q8b[:, :],
                       wdqb[:, :], [("q8",), ("q8b",), ("wdq",)])
            for j in range(2):
                oc = oc0 + j
                wsl = wdqb[:, j * KI:(j + 1) * KI]
                p.op("sync",
                     lambda e, oc=oc, wsl=wsl: e.dma_start(
                         out=asm_3d[:, 0:8,
                                    (oc % 4) * 128:(oc % 4 + 1) * 128],
                         in_=wsl.rearrange("p (c q) -> p c q", q=128),
                         transpose=True),
                     reads=[("wdq",)], writes=[("asm", oc % 4)],
                     sem=DQ_T, inc=16)
            oc = oc0 + 1
            if oc % 4 == 3:
                ocol = oc // 4
                p.op("scalar",
                     lambda e, ocol=ocol: e.dma_start(
                         out=scr_d_my[ocol * 8:(ocol + 1) * 8].rearrange(
                             "c p q -> p c q"),
                         in_=asm_3d[:, 0:8, :]),
                     reads=[("asm", 0), ("asm", 1), ("asm", 2), ("asm", 3)],
                     writes=[("scrmy", "d")], sem=DQ_S, inc=16)
            if bg is not None and ob % 2 == 1:
                next(bg, None)
        p.op("gpsimd",
             lambda e: e.collective_compute(
                 "AllGather", OP.bypass, replica_groups=allg,
                 ins=[scr_d_my.ap().opt()], outs=[scr_d.ap().opt()]),
             reads=[("scrmy", "d")], writes=["scr_d"], sem=CC, inc=1)

    pnlA_3d = pnlA.rearrange("p (c q) -> p c q", q=512)
    pnlB_3d = pnlB.rearrange("p (c q) -> p c q", q=512)

    def ffn_chunk(k, bg=None, store=False):
        def load_gu(pi, icol, dst3d, key):
            row0 = k * 32 + pi * 16
            war = [key] + ([("asm", j) for j in range(4)]
                           if key == ("pnlB",) else [])
            p.op("gpsimd",
                 lambda e, pi=pi, icol=icol, row0=row0: e.dma_start(
                     out=dst3d,
                     in_=scr_all[icol][row0:row0 + 16].rearrange(
                         "c p q -> p c q")),
                 reads=[("scr", icol)], writes=[key], war=war,
                 sem=(DQ_GA if key == ("pnlA",) else DQ_GB), inc=16)

        for icol in range(2):
            load_gu(0, icol, pnlA_3d, ("pnlA",))
            load_gu(1, icol, pnlB_3d, ("pnlB",))
            for tb in range(TT):
                p.group(
                    "tensor",
                    [lambda e, fc=fc, tb=tb: e.matmul(
                        psh(tb), xdqT_3d[:, fc, tb * 128:(tb + 1) * 128],
                        pnlA_3d[:, fc, :],
                        start=(fc == 0), stop=(fc == FC - 1))
                     for fc in range(FC)],
                    reads=[("xdqT", tb), ("pnlA",)],
                    writes=[("ps", tb)], war=[("ps", tb)])
                p.op("scalar",
                     lambda e, tb=tb: e.activation(
                         gate_s[:, tb * 512:(tb + 1) * 512], psh(tb),
                         AF.Silu),
                     reads=[("ps", tb)], writes=[("gs", tb)])
            for tb in range(TT):
                p.group(
                    "tensor",
                    [lambda e, fc=fc, tb=tb: e.matmul(
                        psh(tb), xdqT_3d[:, fc, tb * 128:(tb + 1) * 128],
                        pnlB_3d[:, fc, :],
                        start=(fc == 0), stop=(fc == FC - 1))
                     for fc in range(FC)],
                    reads=[("xdqT", tb), ("pnlB",)],
                    writes=[("ps", tb)], war=[("ps", tb)])
                p.op("vector",
                     lambda e, tb=tb: e.tensor_tensor(
                         gate_s[:, tb * 512:(tb + 1) * 512], psh(tb),
                         gate_s[:, tb * 512:(tb + 1) * 512], op=OP.mult),
                     reads=[("ps", tb), ("gs", tb)], writes=[("gs", tb)])
                if tb % 4 == 3:
                    tbb = tb // 4
                    gsl = gate_s[:, tbb * 2048:(tbb + 1) * 2048]
                    quant_tile(gsl,
                               [("gs", 4 * tbb), ("gs", 4 * tbb + 1),
                                ("gs", 4 * tbb + 2), ("gs", 4 * tbb + 3)],
                               2048, tbb % 2, q8[:, :], q8b[:, :],
                               wdqb[:, :], [("q8",), ("q8b",), ("wdq",)])
                    for j in range(4):
                        tbx = 4 * tbb + j
                        p.op("sync",
                             lambda e, j=j, tbx=tbx, icol=icol: e.dma_start(
                                 out=hdqT_3d[:, icol * 4:(icol + 1) * 4,
                                             tbx * 128:(tbx + 1) * 128],
                                 in_=wdqb[:, j * 512:(j + 1) * 512]
                                 .rearrange("p (c q) -> p c q", q=128),
                                 transpose=True),
                             reads=[("wdq",)],
                             writes=[("hdqT", icol, tbb)],
                             sem=DQ_T, inc=16)
        for half, (pnl3d, pk) in enumerate(((pnlA_3d, ("pnlA",)),
                                            (pnlB_3d, ("pnlB",)))):
            p.op("gpsimd",
                 lambda e, half=half, pnl3d=pnl3d: e.dma_start(
                     out=pnl3d,
                     in_=scr_d[k * 32 + half * 16:
                               k * 32 + (half + 1) * 16].rearrange(
                         "c p q -> p c q")),
                 reads=["scr_d"], writes=[pk],
                 war=[pk] + [("asm", j) for j in range(4)],
                 sem=(DQ_GA if pk == ("pnlA",) else DQ_GB), inc=16)
            for ocol in (2 * half, 2 * half + 1):
                for tb in range(TT):
                    hkeys = [("hdqT", 0, tb // 4), ("hdqT", 1, tb // 4)]
                    p.group(
                        "tensor",
                        [lambda e, ic=ic, tb=tb, ocol=ocol, pnl3d=pnl3d:
                         e.matmul(
                            psh(tb), hdqT_3d[:, ic, tb * 128:(tb + 1) * 128],
                            pnl3d[:, (ocol % 2) * 8 + ic, :],
                            start=(ic == 0), stop=(ic == 7))
                         for ic in range(8)],
                        reads=hkeys + [pk],
                        writes=[("ps", tb)], war=[("ps", tb)])
                    sl = slice(tb * DM + ocol * 512,
                               tb * DM + (ocol + 1) * 512)
                    p.op("vector",
                         lambda e, tb=tb, sl=sl: e.scalar_tensor_tensor(
                             x_res[:, sl], psh(tb), 0.5, x_res[:, sl],
                             OP.mult, OP.add),
                         reads=[("ps", tb), ("x", tb)], writes=[("x", tb)])
                    if ocol == 3:
                        if bg is not None:
                            next(bg, None)
                        if store:
                            y3 = y_out.rearrange("(q t) f -> q t f", t=TT)
                            p.op("scalar",
                                 lambda e, tb=tb, y3=y3: e.dma_start(
                                     out=y3[:, tb, :],
                                     in_=x_res[:, tb * DM:(tb + 1) * DM]),
                                 reads=[("x", tb)], writes=[("y", tb)],
                                 sem=DQ_X, inc=16)

    xnT_3d = xdqT_3d

    def ln2_gen():
        for tb in range(TT):
            layernorm_tile(tb)
            p.op("vector", lambda e: e.tensor_copy(wdqb[:, :], xnb[:, :]),
                 reads=["xnb"], writes=[("wdq",)])
            p.op("sync",
                 lambda e, tb=tb: e.dma_start(
                     out=xdqT_3d[:, :, tb * 128:(tb + 1) * 128],
                     in_=wdqb[:, :].rearrange("p (c q) -> p c q", q=128),
                     transpose=True),
                 reads=[("wdq",)], writes=[("xdqT", tb)], sem=DQ_T, inc=16)
            yield

    def attention(bg=None):
        xnT_keys = [("xdqT", tb) for tb in range(TT)]

        for wsrc, bias, bkey, name in ((wk, bk_s, "bk_s", "k"),
                                       (wv, bv_s, "bv_s", "v"),
                                       (wq, bq_s, "bq_s", "q")):
            for oc in range(16):
                ci = oc % 2
                p.op("scalar",
                     lambda e, wsrc=wsrc, oc=oc, ci=ci: e.dma_start(
                         out=wcol[ci], in_=wsrc[oc]),
                     writes=[("wcol", ci)], war=[("wcol", ci)],
                     sem=(DQ_GC if ci else DQ_G), inc=16)
                p.op("scalar",
                     lambda e, ci=ci: e.copy(wcolb[ci], wcol[ci]),
                     reads=[("wcol", ci)], writes=[("wcolb", ci)])
                for tg in range(2):
                    jps = (oc * 2 + tg) % 8
                    mm = [lambda e, fc=fc, tg=tg, ci=ci, jps=jps: e.matmul(
                        psh(jps),
                        wcolb[ci][:, fc * 128:(fc + 1) * 128],
                        xnT_3d[:, fc, tg * 512:(tg + 1) * 512],
                        start=(fc == 0), stop=(fc == FC - 1))
                        for fc in range(FC)]
                    p.group("tensor", mm,
                            reads=xnT_keys + [("wcolb", ci)],
                            writes=[("ps", jps)], war=[("ps", jps)])
                    if name == "q":
                        p.op("vector",
                             lambda e, oc=oc, tg=tg, jps=jps, bias=bias:
                             e.tensor_scalar(
                                 qt_buf[:, oc * T + tg * 512:
                                        oc * T + (tg + 1) * 512],
                                 psh(jps), bias[:, oc:oc + 1],
                                 None, OP.add),
                             reads=[("ps", jps), bkey], writes=[("qt", oc)])
                    elif name == "k":
                        p.op("vector",
                             lambda e, tg=tg, jps=jps, bias=bias, oc=oc:
                             e.tensor_scalar(
                                 kstage[:, tg * 512:(tg + 1) * 512],
                                 psh(jps), bias[:, oc:oc + 1],
                                 None, OP.add),
                             reads=[("ps", jps), bkey], writes=[("kst", tg)])
                    else:
                        p.op("vector",
                             lambda e, tg=tg, jps=jps, bias=bias, oc=oc:
                             e.tensor_scalar(
                                 vstage[:, tg * 512:(tg + 1) * 512],
                                 psh(jps), bias[:, oc:oc + 1],
                                 None, OP.add),
                             reads=[("ps", jps), bkey], writes=[("vst", tg)])
                if name == "k":
                    p.op("scalar",
                         lambda e, oc=oc: e.dma_start(
                             out=kt_bounce[oc * 128:(oc + 1) * 128, :],
                             in_=kstage),
                         reads=[("kst", 0), ("kst", 1)], writes=[("ktb", oc)],
                         sem=DQ_GKB, inc=16)
                elif name == "v":
                    p.op("sync",
                         lambda e: e.dma_start(
                             out=vtr.rearrange("p (j q) -> p j q", q=128),
                             in_=vstage.rearrange("p (j q) -> p j q", q=128),
                             transpose=True),
                         reads=[("vst", 0), ("vst", 1)], writes=["vtr"],
                         war=["vtr"], sem=DQ_T, inc=16)
                    p.op("scalar",
                         lambda e, oc=oc: e.dma_start(
                             out=v_bounce[:, oc * 128:(oc + 1) * 128]
                             .rearrange("(j p) q -> p j q", p=128),
                             in_=vtr.rearrange("p (j q) -> p j q", q=128)),
                         reads=["vtr"], writes=[("vb", oc)], sem=DQ_GVB, inc=16)
            groups = [[0, 1, 2, 3], [4, 5, 6, 7]]
            if name == "k":
                p.op("gpsimd",
                     lambda e: e.collective_compute(
                         "AllGather", OP.bypass, replica_groups=groups,
                         ins=[kt_bounce.ap().opt()], outs=[kt_all.ap().opt()]),
                     reads=[("ktb", oc) for oc in range(16)],
                     writes=["kt_all"], sem=CC, inc=1)
            elif name == "v":
                p.op("gpsimd",
                     lambda e: e.collective_compute(
                         "AllGather", OP.bypass, replica_groups=groups,
                         ins=[v_bounce.ap().opt()], outs=[v_all.ap().opt()]),
                     reads=[("vb", oc) for oc in range(16)],
                     writes=["v_all"], sem=CC, inc=1)

        p.barrier()

        def load_head(h, i):
            for r in range(RANKS):
                p.op("gpsimd",
                     lambda e, h=h, r=r, i=i: e.dma_start(
                         out=kbuf[i][:, r * T:(r + 1) * T],
                         in_=kt_all[r * DM + h * 128:
                                    r * DM + (h + 1) * 128, :]),
                     reads=["kt_all"], writes=[("kbuf", i)],
                     war=[("kbuf", i)], sem=(DQ_GK if i else DQ_G), inc=16)
                p.op("gpsimd",
                     lambda e, h=h, r=r, i=i: e.dma_start(
                         out=vbuf[i].rearrange(
                             "p (c q) -> p c q", q=128)[:, r * 8:(r + 1) * 8,
                                                        :],
                         in_=v_all[r * T:(r + 1) * T,
                                   h * 128:(h + 1) * 128].rearrange(
                             "(c p) q -> p c q", p=128)),
                     reads=["v_all"], writes=[("vbuf", i)],
                     war=[("vbuf", i)], sem=(DQ_GV if i else DQ_GVB), inc=16)

        load_head(0, 0)
        NCH = S // 128
        NB2 = NCH // 2
        psA_keys = [("ps", j) for j in range(4)]
        psO_keys = [("ps", 4), ("ps", 5)]
        psS_keys = [("ps", 6), ("ps", 7)]
        for h in range(NH):
            i = h % 2
            if h + 1 < NH:
                load_head(h + 1, (h + 1) % 2)

            def av_batch(bc):
                c0, c1 = 2 * bc, 2 * bc + 1
                mm = []
                for cj, c in enumerate((c0, c1)):
                    for tg in range(2):
                        mm.append(lambda e, c=c, cj=cj, tg=tg, i=i, bc=bc:
                                  e.matmul(
                                      psO[:, tg * 512:(tg + 1) * 512],
                                      vbuf[i][:, c * 128:(c + 1) * 128],
                                      pt[bc % 3][:, cj * 1024 + tg * 512:
                                                 cj * 1024 + (tg + 1) * 512],
                                      start=(c == 0), stop=(c == NCH - 1)))
                p.group("tensor", mm,
                        reads=[("pt", bc % 3), ("vbuf", i)],
                        writes=psO_keys if c1 == NCH - 1 else [],
                        war=psO_keys if bc == 0 else [])

            def sum_batch(bc):
                c0, c1 = 2 * bc, 2 * bc + 1
                mm = []
                for cj, c in enumerate((c0, c1)):
                    for tg in range(2):
                        mm.append(lambda e, c=c, cj=cj, tg=tg, bc=bc:
                                  e.matmul(
                                      psS[0:1, tg * 512:(tg + 1) * 512],
                                      ones_bf[:, 0:1],
                                      pt[bc % 3][:, cj * 1024 + tg * 512:
                                                 cj * 1024 + (tg + 1) * 512],
                                      start=(c == 0), stop=(c == NCH - 1)))
                p.group("tensor", mm,
                        reads=[("pt", bc % 3)],
                        writes=psS_keys if c1 == NCH - 1 else [],
                        war=psS_keys if bc == 0 else [])

            for bc in range(NB2):
                c0, c1 = 2 * bc, 2 * bc + 1
                mm = []
                for cj, c in enumerate((c0, c1)):
                    for tg in range(2):
                        mm.append(lambda e, c=c, tg=tg, cj=cj, i=i, h=h:
                                  e.matmul(
                                      psA[:, cj * 1024 + tg * 512:
                                          cj * 1024 + (tg + 1) * 512],
                                      kbuf[i][:, c * 128:(c + 1) * 128],
                                      qt_buf[:, h * T + tg * 512:
                                             h * T + (tg + 1) * 512],
                                      start=True, stop=True))
                p.group("tensor", mm, reads=[("kbuf", i), ("qt", h)],
                        writes=psA_keys, war=psA_keys)
                p.op("scalar",
                     lambda e, bc=bc: e.activation(
                         pt[bc % 3][:, :], psA[:, :], AF.Exp,
                         scale=SCALE_QK),
                     reads=psA_keys, writes=[("pt", bc % 3)])
                if bc >= 2:
                    sum_batch(bc - 2)
                if bc >= 1:
                    av_batch(bc - 1)
            av_batch(NB2 - 1)
            sum_batch(NB2 - 2)
            sum_batch(NB2 - 1)
            p.op("scalar", lambda e: e.copy(otstage, psO[:, :]),
                 reads=psO_keys, writes=["otb"], war=["otb"])
            p.op("vector", lambda e: e.tensor_copy(serow[0:1, :],
                                                   psS[0:1, :]),
                 reads=psS_keys, writes=["serow"], war=["serow"])
            p.op("vector", lambda e: e.reciprocal(serec[0:1, :],
                                                  serow[0:1, :]),
                 reads=["serow"], writes=["serec"])
            p.group("tensor",
                    [lambda e, tg=tg: e.matmul(
                        psS[:, tg * 512:(tg + 1) * 512],
                        onesf[0:1, 0:128],
                        serec[0:1, tg * 512:(tg + 1) * 512],
                        start=True, stop=True) for tg in range(2)],
                    reads=["serec", "onesrow"], writes=psS_keys,
                    war=psS_keys)
            p.op("scalar", lambda e: e.copy(serbs, psS[:, :]),
                 reads=psS_keys, writes=["serbs"])
            p.op("vector",
                 lambda e: e.tensor_tensor(otstage, otstage, serbs,
                                           op=OP.mult),
                 reads=["otb", "serbs"], writes=["otb"])
            p.op("scalar",
                 lambda e, h=h: e.dma_start(out=ot_scr[h], in_=otstage),
                 reads=["otb"], writes=[("ot", h)], sem=DQ_S, inc=16)

        p.barrier()

        for ocol in range(4):
            for dc in range(16):
                di = dc % 2
                p.op("gpsimd",
                     lambda e, dc=dc, di=di: e.dma_start(
                         out=otrow[di], in_=ot_scr[dc]),
                     reads=[("ot", dc)], writes=[("otrow", di)],
                     war=[("otrow", di)], sem=(DQ_GO if di else DQ_G), inc=16)
                p.op("gpsimd",
                     lambda e, dc=dc, ocol=ocol, di=di: e.dma_start(
                         out=wrow[di],
                         in_=wo[dc][:, ocol * 512:(ocol + 1) * 512]),
                     writes=[("wrow", di)], war=[("wrow", di)],
                     sem=(DQ_GR if di else DQ_GKB), inc=16)
                p.op("vector",
                     lambda e, di=di: e.tensor_copy(wrowb[di], wrow[di]),
                     reads=[("wrow", di)], writes=[("wrowb", di)])
                for tb in range(TT):
                    p.group(
                        "tensor",
                        [lambda e, dc=dc, tb=tb, di=di: e.matmul(
                            psh(tb),
                            otrow[di][:, tb * 128:(tb + 1) * 128],
                            wrowb[di], start=(dc == 0), stop=False)],
                        reads=[("otrow", di), ("wrowb", di)],
                        writes=[],
                        war=[("ps", tb)] if dc == 0 else [],
                    )
            for tb in range(TT):
                p.group(
                    "tensor",
                    [lambda e, tb=tb, ocol=ocol: e.matmul(
                        psh(tb), ones_bf[0:1, 0:128],
                        bo_bf[0:1, ocol * 512:(ocol + 1) * 512],
                        start=False, stop=True)],
                    reads=["ones_bf", "borow"],
                    writes=[("ps", tb)],
                )
            for tb in range(TT):
                sl = slice(tb * DM + ocol * 512, tb * DM + (ocol + 1) * 512)
                p.op("vector",
                     lambda e, tb=tb, sl=sl: e.scalar_tensor_tensor(
                         x_res[:, sl], psh(tb), 1.0, x_res[:, sl],
                         OP.mult, OP.add),
                     reads=[("ps", tb), ("x", tb)], writes=[("x", tb)])
                if bg is not None and ocol == 3:
                    next(bg, None)

    _bg = ln_and_quant_x_gen()
    quant_my_weights(bg=_bg)
    for _ in _bg:
        pass
    _ln2 = ln2_gen()
    for k in range(KS):
        ffn_chunk(k, bg=_ln2 if k == KS - 1 else None)
    for _ in _ln2:
        pass
    p.barrier()
    _bg2 = ln_and_quant_x_gen()
    attention(bg=_bg2)
    for _ in _bg2:
        pass
    p.barrier()
    for k in range(KS):
        ffn_chunk(k, store=(k == KS - 1))
    p.barrier()

    p.emit(nc, es)
    es.close()
    return nc


def kernel(L0_attn_x, L0_attn_Wq, L0_attn_bq, L0_attn_Wk, L0_attn_bk,
           L0_attn_Wv, L0_attn_bv, L0_attn_Wout, L0_attn_bout,
           L0_ffn_gate_w, L0_ffn_up_w, L0_ffn_down_w):
    x = np.ascontiguousarray(
        np.asarray(L0_attn_x, dtype=np.float32).reshape(2 * S, DM))
    gw = np.asarray(L0_ffn_gate_w, np.float32)
    uw = np.asarray(L0_ffn_up_w, np.float32)
    dwf = np.asarray(L0_ffn_down_w, np.float32)

    def blk(W):
        W = np.asarray(W, dtype=np.float32)
        return np.ascontiguousarray(
            W.reshape(16, 128, 16, 128).transpose(0, 3, 2, 1)
        ).reshape(16, 128, DM)

    def col(b):
        return np.ascontiguousarray(
            np.asarray(b, dtype=np.float32).reshape(16, 128).T)

    common = {
        "wq": blk(L0_attn_Wq), "wk": blk(L0_attn_Wk), "wv": blk(L0_attn_Wv),
        "wo": np.ascontiguousarray(
            np.asarray(L0_attn_Wout, np.float32).reshape(DM, 16, 128)
            .transpose(1, 2, 0)),
        "bqc": col(L0_attn_bq), "bkc": col(L0_attn_bk),
        "bvc": col(L0_attn_bv),
        "bo": np.ascontiguousarray(
            np.asarray(L0_attn_bout, np.float32).reshape(1, DM)),
    }
    in_maps = []
    for c in range(NCORES):
        m = dict(common)
        m["x"] = np.ascontiguousarray(x[c * T:(c + 1) * T])
        m["gw_my"] = np.ascontiguousarray(gw[c * KI:(c + 1) * KI])
        m["uw_my"] = np.ascontiguousarray(uw[c * KI:(c + 1) * KI])
        m["dw_my"] = np.ascontiguousarray(dwf[:, c * KI:(c + 1) * KI])
        in_maps.append(m)

    nc = build()
    res = run_bass_kernel_spmd(nc, in_maps, core_ids=list(range(NCORES)))
    y = np.concatenate([res.results[c]["y"] for c in range(NCORES)], axis=0)
    return y.reshape(2, S, DM)


# version 17
# speedup vs baseline: 1.0622x; 1.0051x over previous
import numpy as np
from contextlib import ExitStack

import concourse.bass as bass
import concourse.mybir as mybir
from concourse.bass_utils import run_bass_kernel_spmd

F32 = mybir.dt.float32
BF16 = mybir.dt.bfloat16
FP8 = mybir.dt.float8e4
U32 = mybir.dt.uint32

AF = mybir.ActivationFunctionType
OP = mybir.AluOpType
AX = mybir.AxisListType

NCORES = 8
T = 1024
TT = 8
DM = 2048
FC = 16
DI = 8192
KS = 8
KI = 1024
NH = 16
S = 4096
RANKS = 4
EPS = 1e-5
SCALE_QK = float(1.0 / np.sqrt(128.0))


class Prog:

    ENGS = ("tensor", "vector", "scalar", "gpsimd", "sync")

    def __init__(self):
        self.sem_count = {}
        self.sem_handles = {}
        self.last_w = {}
        self.readers = {}
        self.clock = {e: {} for e in self.ENGS}
        self.stream = {e: [] for e in self.ENGS}
        self.sem_names = set()

    def sem(self, name):
        self.sem_names.add(name)
        self.sem_count.setdefault(name, 0)
        return name

    def _need(self, eng, dep):
        s, v, snap = dep
        waits = []
        c = self.clock[eng]
        if c.get(s, 0) < v:
            waits.append((s, v))
            c[s] = v
            if snap:
                for ss, vv in snap.items():
                    if c.get(ss, 0) < vv:
                        c[ss] = vv
        return waits

    def _gather(self, eng, reads, writes, war):
        waits = []
        for k in reads:
            d = self.last_w.get(k)
            if d is not None:
                waits += self._need(eng, d)
        for k in tuple(writes) + tuple(war):
            d = self.last_w.get(k)
            if d is not None:
                waits += self._need(eng, d)
            for d in self.readers.get(k, ()):
                waits += self._need(eng, d)
        return waits

    def op(self, eng, fn, reads=(), writes=(), war=(), sem=None, inc=1):
        return self.group(eng, [fn], reads, writes, war, sem, inc)

    def group(self, eng, fns, reads=(), writes=(), war=(), sem=None, inc=1):
        waits = self._gather(eng, reads, writes, war)
        s = sem if sem is not None else eng
        self.sem_names.add(s)
        self.sem_count[s] = self.sem_count.get(s, 0) + inc
        val = self.sem_count[s]
        for i, fn in enumerate(fns):
            self.stream[eng].append(
                (waits if i == 0 else [], fn,
                 s if i == len(fns) - 1 else None, inc))
        dep = (s, val, dict(self.clock[eng]))
        for k in writes:
            self.last_w[k] = dep
            self.readers[k] = []
        for k in reads:
            self.readers.setdefault(k, []).append(dep)
        return dep

    def barrier(self):
        counts = dict(self.sem_count)
        for e in self.ENGS:
            waits = []
            for s, v in counts.items():
                if v > 0 and self.clock[e].get(s, 0) < v:
                    waits.append((s, v))
                    self.clock[e][s] = v
            if waits:
                self.stream[e].append((waits, None, None, 0))

    def emit(self, nc, es):
        for s in sorted(self.sem_names):
            self.sem_handles[s] = es.enter_context(nc.semaphore(s))
        block = es.enter_context(nc.Block())
        prog = self

        def runner(ename):
            def run(engine):
                for waits, fn, s, inc in prog.stream[ename]:
                    for ws, wv in waits:
                        engine.wait_ge(prog.sem_handles[ws], wv)
                    if fn is not None:
                        instr = fn(engine)
                        if s is not None:
                            instr.then_inc(prog.sem_handles[s], inc)
            return run

        block.tensor(runner("tensor"))
        block.vector(runner("vector"))
        block.scalar(runner("scalar"))
        block.gpsimd(runner("gpsimd"))
        block.sync(runner("sync"))


def build():
    nc = bass.Bass("TRN2", target_bir_lowering=False)
    p = Prog()

    x_in = nc.dram_tensor("x", [T, DM], F32, kind="ExternalInput")
    gw_my = nc.dram_tensor("gw_my", [KI, DM], F32, kind="ExternalInput")
    uw_my = nc.dram_tensor("uw_my", [KI, DM], F32, kind="ExternalInput")
    dw_my = nc.dram_tensor("dw_my", [DM, KI], F32, kind="ExternalInput")
    wq = nc.dram_tensor("wq", [16, 128, DM], F32, kind="ExternalInput")
    wk = nc.dram_tensor("wk", [16, 128, DM], F32, kind="ExternalInput")
    wv = nc.dram_tensor("wv", [16, 128, DM], F32, kind="ExternalInput")
    wo = nc.dram_tensor("wo", [16, 128, DM], F32, kind="ExternalInput")
    bqc = nc.dram_tensor("bqc", [128, 16], F32, kind="ExternalInput")
    bkc = nc.dram_tensor("bkc", [128, 16], F32, kind="ExternalInput")
    bvc = nc.dram_tensor("bvc", [128, 16], F32, kind="ExternalInput")
    bo = nc.dram_tensor("bo", [1, DM], F32, kind="ExternalInput")
    y_out = nc.dram_tensor("y", [T, DM], F32, kind="ExternalOutput")

    scr_my = {}
    scr_all = {}
    for icol in range(2):
        scr_my[icol] = nc.dram_tensor(
            f"scr_my_{icol}", [32, 128, 512], BF16)
        scr_all[icol] = nc.dram_tensor(
            f"scr_all_{icol}", [KS * 32, 128, 512], BF16,
            addr_space="Shared")
    scr_d_my = nc.dram_tensor("scr_d_my", [32, 128, 512], BF16)
    scr_d = nc.dram_tensor("scr_d", [KS * 32, 128, 512], BF16,
                           addr_space="Shared")
    ot_scr = nc.dram_tensor("ot_scr", [NH, 128, T], BF16)
    kt_bounce = nc.dram_tensor("kt_bounce", [DM, T], BF16)
    v_bounce = nc.dram_tensor("v_bounce", [T, DM], BF16)
    kt_all = nc.dram_tensor("kt_all", [RANKS * DM, T], BF16)
    v_all = nc.dram_tensor("v_all", [RANKS * T, DM], BF16)

    es = ExitStack()

    def sb(name, shape, dt):
        return es.enter_context(nc.sbuf_tensor(name, shape, dt))

    x_res = sb("x_res", [128, TT * DM], F32)
    bo_bf = sb("bo_bf", [1, DM], BF16)
    ones_bf = sb("ones_bf", [128, 128], BF16)
    onesf = sb("onesf", [1, 512], F32)
    bq_s = sb("bq_s", [128, 16], F32)
    bk_s = sb("bk_s", [128, 16], F32)
    bv_s = sb("bv_s", [128, 16], F32)
    xnb = sb("xnb", [128, DM], F32)
    stat = sb("stat", [128, 16], F32)
    wq8f = sb("wq8f", [128, 4096], FP8)
    wq8bf = wq8f
    wdq4 = sb("wdq4", [128, 4096], BF16)
    q8 = wq8f[:, 0:2048]
    q8b = q8
    wdqb = wdq4[:, 0:2048]
    mab = [sb(f"mab{i}", [128, 128], F32) for i in range(2)]
    sc2b = [sb(f"sc2b{i}", [128, 128], F32) for i in range(2)]
    rc2b = [sb(f"rc2b{i}", [128, 128], F32) for i in range(2)]

    arF = sb("arF", [128, 8192], F32)
    arB = sb("arB", [128, 40960], BF16)

    def cF(off, n):
        return arF[:, off:off + n]

    def cB(off, n):
        return arB[:, off:off + n]

    gate_s = cF(0, 4096)
    wfb = [cF(4096 + i * 2048, 2048) for i in range(2)]
    xdqT = cB(0, FC * T)
    hdqT = cB(16384, KI * TT)
    pnlA = cB(24576, 8192)
    pnlB = cB(32768, 8192)
    asm = cB(32768, 8192)
    wcol = [cF(0 + i * 2048, 2048) for i in range(2)]
    qt_buf = cB(16384, NH * T)
    wcolb = [cB(32768 + i * 2048, 2048) for i in range(2)]
    kstage = cB(36864, 1024)
    vstage = cB(37888, 1024)
    vtr = cB(38912, 1024)
    kbuf = [cB(0 + i * S, S) for i in range(2)]
    vbuf = [cB(8192 + i * S, S) for i in range(2)]
    pt = [cB(32768 + i * 2048, 2048) for i in range(3)]
    otstage = cB(38912, 1024)
    serow = cF(4096, 1024)
    serec = cF(5120, 1024)
    serbs = cF(6144, 1024)
    otrow = [cB(16384 + i * 1024, 1024) for i in range(2)]
    wrowb = [cB(18432 + i * 512, 512) for i in range(2)]
    wrow = [cF(7168 + i * 512, 512) for i in range(2)]

    psA = es.enter_context(nc.psum_tensor("psA", [128, 2048], F32))
    psO = es.enter_context(nc.psum_tensor("psO", [128, 1024], F32))
    psS = es.enter_context(nc.psum_tensor("psS", [128, 1024], F32))

    def psh(j):
        if j < 4:
            return psA[:, j * 512:(j + 1) * 512]
        if j < 6:
            return psO[:, (j - 4) * 512:(j - 3) * 512]
        return psS[:, (j - 6) * 512:(j - 5) * 512]

    DQ_W = p.sem("dq_w")
    DQ_T = p.sem("dq_t")
    DQ_P = p.sem("dq_p")
    DQ_S = p.sem("dq_s")
    DQ_X = p.sem("dq_x")
    DQ_K = p.sem("dq_k")
    DQ_G = p.sem("dq_g")
    DQ_GA = p.sem("dq_ga")
    DQ_GB = p.sem("dq_gb")
    DQ_GW2 = p.sem("dq_gw2")
    DQ_GK = p.sem("dq_gk")
    DQ_GV = p.sem("dq_gv")
    DQ_GC = p.sem("dq_gc")
    DQ_GO = p.sem("dq_go")
    DQ_GR = p.sem("dq_gr")
    DQ_GKB = p.sem("dq_gkb")
    DQ_GVB = p.sem("dq_gvb")
    CC = p.sem("cc")

    p.op("vector", lambda e: e.memset(ones_bf[:, :], 1.0), writes=["ones_bf"])
    p.op("vector", lambda e: e.memset(onesf[:, :], 1.0), writes=["onesrow"])
    p.op("gpsimd", lambda e: e.dma_start(out=bo_bf[:, :], in_=bo[:, :]),
         writes=["borow"], sem=DQ_G, inc=16)
    p.op("scalar", lambda e: e.dma_start(out=bq_s[:, :], in_=bqc[:, :]),
         writes=["bq_s"], sem=DQ_X, inc=16)
    p.op("scalar", lambda e: e.dma_start(out=bk_s[:, :], in_=bkc[:, :]),
         writes=["bk_s"], sem=DQ_X, inc=16)
    p.op("scalar", lambda e: e.dma_start(out=bv_s[:, :], in_=bvc[:, :]),
         writes=["bv_s"], sem=DQ_X, inc=16)
    p.op("gpsimd",
         lambda e: e.dma_start(
             out=x_res[:, :].rearrange("p (t f) -> p t f", f=DM),
             in_=x_in[:, :].rearrange("(t p) f -> t p f", p=128)),
         writes=[("x", tb) for tb in range(TT)], sem=DQ_GR, inc=16)

    def quant_tile(src_ap, src_keys, W, i, dst_q8, dst_q8b, dst_dq, keys):
        nb = W // 32
        ma = mab[i][:, 0:nb]
        sc2 = sc2b[i][:, 0:nb]
        rc2 = rc2b[i][:, 0:nb]
        kma, ksc, krc = ("ma", i), ("sc", i), ("rc", i)
        src3 = src_ap.rearrange("p (n b) -> p n b", b=32)
        p.op("vector",
             lambda e: e.tensor_reduce(ma.unsqueeze(2), src3, axis=AX.X,
                                       op=OP.max, apply_absolute_value=True),
             reads=src_keys, writes=[kma])
        p.op("vector",
             lambda e: e.tensor_scalar(sc2, ma, 2.0 / 448.0, 2e-30,
                                       OP.mult, OP.max),
             reads=[kma], writes=[ksc])
        p.op("vector", lambda e: e.reciprocal(rc2, sc2),
             reads=[ksc], writes=[krc])
        rcb = rc2.unsqueeze(2).broadcast_to((128, nb, 32))
        p.op("vector",
             lambda e: e.tensor_tensor(
                 dst_q8.rearrange("p (n b) -> p n b", b=32), src3, rcb,
                 op=OP.mult),
             reads=src_keys + [krc], writes=[keys[0]])
        p.op("vector",
             lambda e: e.tensor_copy(dst_q8b.bitcast(U32), dst_q8.bitcast(U32)),
             reads=[keys[0]], writes=[keys[1]])
        scb = sc2.unsqueeze(2).broadcast_to((128, nb, 32))
        p.op("vector",
             lambda e: e.tensor_tensor(
                 dst_dq.rearrange("p (n b) -> p n b", b=32),
                 dst_q8b.rearrange("p (n b) -> p n b", b=32), scb,
                 op=OP.mult),
             reads=[keys[1], ksc], writes=[keys[2]])

    def layernorm_tile(tb):
        xt = x_res[:, tb * DM:(tb + 1) * DM]
        xk = [("x", tb)]
        p.op("scalar",
             lambda e: e.activation(xnb[:, :], xt, AF.Square,
                                    accum_out=stat[:, 1:2]),
             reads=xk, writes=["xnb", "st1"])
        p.op("vector",
             lambda e: e.tensor_reduce(
                 stat[:, 0:1].unsqueeze(2),
                 xt.rearrange("p (a f) -> p a f", a=1), axis=AX.X, op=OP.add),
             reads=xk, writes=["st0"])
        p.op("vector",
             lambda e: e.tensor_scalar(stat[:, 2:3], stat[:, 0:1],
                                       -1.0 / DM, None, OP.mult),
             reads=["st0"], writes=["st2"])
        p.op("vector",
             lambda e: e.tensor_scalar(stat[:, 3:4], stat[:, 1:2],
                                       1.0 / DM, None, OP.mult),
             reads=["st1"], writes=["st3"])
        p.op("vector",
             lambda e: e.scalar_tensor_tensor(stat[:, 5:6], stat[:, 2:3],
                                              stat[:, 2:3], stat[:, 3:4],
                                              OP.mult, OP.subtract),
             reads=["st2", "st3"], writes=["st5"])
        p.op("vector",
             lambda e: e.tensor_scalar(stat[:, 6:7], stat[:, 5:6],
                                       -1.0, EPS, OP.mult, OP.add),
             reads=["st5"], writes=["st6"])
        p.op("scalar", lambda e: e.sqrt(stat[:, 7:8], stat[:, 6:7]),
             reads=["st6"], writes=["st7"])
        p.op("vector", lambda e: e.reciprocal(stat[:, 8:9], stat[:, 7:8]),
             reads=["st7"], writes=["st8"])
        p.op("vector",
             lambda e: e.tensor_tensor(stat[:, 9:10], stat[:, 8:9],
                                       stat[:, 8:9], op=OP.mult),
             reads=["st8"], writes=["st9"])
        p.op("vector",
             lambda e: e.tensor_tensor(stat[:, 10:11], stat[:, 9:10],
                                       stat[:, 6:7], op=OP.mult),
             reads=["st9", "st6"], writes=["st10"])
        p.op("vector",
             lambda e: e.tensor_scalar(stat[:, 11:12], stat[:, 10:11],
                                       -0.5, 1.5, OP.mult, OP.add),
             reads=["st10"], writes=["st11"])
        p.op("vector",
             lambda e: e.tensor_tensor(stat[:, 4:5], stat[:, 8:9],
                                       stat[:, 11:12], op=OP.mult),
             reads=["st8", "st11"], writes=["st4"])
        p.op("vector",
             lambda e: e.tensor_scalar(xnb[:, :], xt, stat[:, 2:3],
                                       stat[:, 4:5], OP.add, OP.mult),
             reads=xk + ["st2", "st4"], writes=["xnb"])

    xdqT_3d = xdqT.rearrange("p (fc t) -> p fc t", t=T)
    hdqT_3d = hdqT.rearrange("p (ic t) -> p ic t", t=T)
    asm_3d = asm.rearrange("p (c q) -> p c q", q=512)

    def ln_and_quant_x_gen():
        for tb in range(TT):
            layernorm_tile(tb)
            quant_tile(xnb[:, :], ["xnb"], DM, tb % 2, q8[:, :], q8b[:, :],
                       wdqb[:, :], [("q8",), ("q8b",), ("wdq",)])
            p.op("sync",
                 lambda e, tb=tb: e.dma_start(
                     out=xdqT_3d[:, :, tb * 128:(tb + 1) * 128],
                     in_=wdqb[:, :].rearrange("p (c q) -> p c q", q=128),
                     transpose=True),
                 reads=[("wdq",)], writes=[("xdqT", tb)], sem=DQ_T, inc=16)
            yield

    def ln_and_quant_x(bg=None):
        for _ in ln_and_quant_x_gen():
            pass

    def quant_my_weights(bg=None):
        allg = [[0, 1, 2, 3, 4, 5, 6, 7]]
        wf4 = arF[:, 4096:8192]
        for icol in range(2):
            for pi, src_t in ((0, gw_my), (1, uw_my)):
                for ab in range(2):
                    a0 = icol * 4 + 2 * ab
                    kw, kw2 = ("wf", 0), ("wf", 1)
                    p.op("scalar",
                         lambda e, src_t=src_t, a0=a0: e.dma_start(
                             out=wfb[0],
                             in_=src_t[a0 * 128:(a0 + 1) * 128, :]),
                         writes=[kw], war=[kw], sem=DQ_X, inc=16)
                    p.op("scalar",
                         lambda e, src_t=src_t, a0=a0: e.dma_start(
                             out=wfb[1],
                             in_=src_t[(a0 + 1) * 128:(a0 + 2) * 128, :]),
                         writes=[kw2], war=[kw2], sem=DQ_S, inc=16)
                    quant_tile(wf4, [kw, kw2], 4096, ab % 2, wq8f[:, :],
                               wq8f[:, :], wdq4[:, :],
                               [("q8",), ("q8b",), ("wdq",)])
                    for j in range(2):
                        aa = (a0 + j) % 4
                        wsl = wdq4[:, j * 2048:(j + 1) * 2048]
                        p.op("sync",
                             lambda e, aa=aa, wsl=wsl: e.dma_start(
                                 out=asm_3d[:, :, aa * 128:(aa + 1) * 128],
                                 in_=wsl.rearrange("p (c q) -> p c q", q=128),
                                 transpose=True),
                             reads=[("wdq",)], writes=[("asm", aa)],
                             sem=DQ_T, inc=16)
                p.op("scalar",
                     lambda e, pi=pi, icol=icol: e.dma_start(
                         out=scr_my[icol][pi * 16:(pi + 1) * 16].rearrange(
                             "c p q -> p c q"),
                         in_=asm_3d),
                     reads=[("asm", 0), ("asm", 1), ("asm", 2), ("asm", 3)],
                     writes=[("scrmy", icol, pi)], sem=DQ_S, inc=16)
            p.op("gpsimd",
                 lambda e, icol=icol: e.collective_compute(
                     "AllGather", OP.bypass, replica_groups=allg,
                     ins=[scr_my[icol].ap().opt()],
                     outs=[scr_all[icol].ap().opt()]),
                 reads=[("scrmy", icol, 0), ("scrmy", icol, 1)],
                 writes=[("scr", icol)], sem=CC, inc=1)
            if bg is not None and icol == 0:
                for _ in range(4):
                    next(bg, None)
        for ob in range(8):
            oc0 = 2 * ob
            kw, kw2 = ("wf", 0), ("wf", 1)
            p.op("scalar",
                 lambda e, oc0=oc0: e.dma_start(
                     out=wfb[0][:, 0:KI],
                     in_=dw_my[oc0 * 128:(oc0 + 1) * 128, :]),
                 writes=[kw], war=[kw], sem=DQ_X, inc=16)
            p.op("scalar",
                 lambda e, oc0=oc0: e.dma_start(
                     out=wfb[0][:, KI:2 * KI],
                     in_=dw_my[(oc0 + 1) * 128:(oc0 + 2) * 128, :]),
                 writes=[kw2], war=[kw2], sem=DQ_S, inc=16)
            quant_tile(wfb[0], [kw, kw2], 2048, ob % 2, q8[:, :], q8b[:, :],
                       wdqb[:, :], [("q8",), ("q8b",), ("wdq",)])
            for j in range(2):
                oc = oc0 + j
                wsl = wdqb[:, j * KI:(j + 1) * KI]
                p.op("sync",
                     lambda e, oc=oc, wsl=wsl: e.dma_start(
                         out=asm_3d[:, 0:8,
                                    (oc % 4) * 128:(oc % 4 + 1) * 128],
                         in_=wsl.rearrange("p (c q) -> p c q", q=128),
                         transpose=True),
                     reads=[("wdq",)], writes=[("asm", oc % 4)],
                     sem=DQ_T, inc=16)
            oc = oc0 + 1
            if oc % 4 == 3:
                ocol = oc // 4
                p.op("scalar",
                     lambda e, ocol=ocol: e.dma_start(
                         out=scr_d_my[ocol * 8:(ocol + 1) * 8].rearrange(
                             "c p q -> p c q"),
                         in_=asm_3d[:, 0:8, :]),
                     reads=[("asm", 0), ("asm", 1), ("asm", 2), ("asm", 3)],
                     writes=[("scrmy", "d")], sem=DQ_S, inc=16)
            if bg is not None and ob % 2 == 1:
                next(bg, None)
        p.op("gpsimd",
             lambda e: e.collective_compute(
                 "AllGather", OP.bypass, replica_groups=allg,
                 ins=[scr_d_my.ap().opt()], outs=[scr_d.ap().opt()]),
             reads=[("scrmy", "d")], writes=["scr_d"], sem=CC, inc=1)

    pnlA_3d = pnlA.rearrange("p (c q) -> p c q", q=512)
    pnlB_3d = pnlB.rearrange("p (c q) -> p c q", q=512)

    def ffn_chunk(k, bg=None, store=False):
        def load_gu(pi, icol, dst3d, key):
            row0 = k * 32 + pi * 16
            war = [key] + ([("asm", j) for j in range(4)]
                           if key == ("pnlB",) else [])
            p.op("gpsimd",
                 lambda e, pi=pi, icol=icol, row0=row0: e.dma_start(
                     out=dst3d,
                     in_=scr_all[icol][row0:row0 + 16].rearrange(
                         "c p q -> p c q")),
                 reads=[("scr", icol)], writes=[key], war=war,
                 sem=(DQ_GA if key == ("pnlA",) else DQ_GB), inc=16)

        for icol in range(2):
            load_gu(0, icol, pnlA_3d, ("pnlA",))
            load_gu(1, icol, pnlB_3d, ("pnlB",))
            for tb in range(TT):
                p.group(
                    "tensor",
                    [lambda e, fc=fc, tb=tb: e.matmul(
                        psh(tb), xdqT_3d[:, fc, tb * 128:(tb + 1) * 128],
                        pnlA_3d[:, fc, :],
                        start=(fc == 0), stop=(fc == FC - 1))
                     for fc in range(FC)],
                    reads=[("xdqT", tb), ("pnlA",)],
                    writes=[("ps", tb)], war=[("ps", tb)])
                p.op("scalar",
                     lambda e, tb=tb: e.activation(
                         gate_s[:, tb * 512:(tb + 1) * 512], psh(tb),
                         AF.Silu),
                     reads=[("ps", tb)], writes=[("gs", tb)])
            for tb in range(TT):
                p.group(
                    "tensor",
                    [lambda e, fc=fc, tb=tb: e.matmul(
                        psh(tb), xdqT_3d[:, fc, tb * 128:(tb + 1) * 128],
                        pnlB_3d[:, fc, :],
                        start=(fc == 0), stop=(fc == FC - 1))
                     for fc in range(FC)],
                    reads=[("xdqT", tb), ("pnlB",)],
                    writes=[("ps", tb)], war=[("ps", tb)])
                p.op("vector",
                     lambda e, tb=tb: e.tensor_tensor(
                         gate_s[:, tb * 512:(tb + 1) * 512], psh(tb),
                         gate_s[:, tb * 512:(tb + 1) * 512], op=OP.mult),
                     reads=[("ps", tb), ("gs", tb)], writes=[("gs", tb)])
                if tb % 4 == 3:
                    tbb = tb // 4
                    gsl = gate_s[:, tbb * 2048:(tbb + 1) * 2048]
                    quant_tile(gsl,
                               [("gs", 4 * tbb), ("gs", 4 * tbb + 1),
                                ("gs", 4 * tbb + 2), ("gs", 4 * tbb + 3)],
                               2048, tbb % 2, q8[:, :], q8b[:, :],
                               wdqb[:, :], [("q8",), ("q8b",), ("wdq",)])
                    for j in range(4):
                        tbx = 4 * tbb + j
                        p.op("sync",
                             lambda e, j=j, tbx=tbx, icol=icol: e.dma_start(
                                 out=hdqT_3d[:, icol * 4:(icol + 1) * 4,
                                             tbx * 128:(tbx + 1) * 128],
                                 in_=wdqb[:, j * 512:(j + 1) * 512]
                                 .rearrange("p (c q) -> p c q", q=128),
                                 transpose=True),
                             reads=[("wdq",)],
                             writes=[("hdqT", icol, tbb)],
                             sem=DQ_T, inc=16)
        for half, (pnl3d, pk) in enumerate(((pnlA_3d, ("pnlA",)),
                                            (pnlB_3d, ("pnlB",)))):
            p.op("gpsimd",
                 lambda e, half=half, pnl3d=pnl3d: e.dma_start(
                     out=pnl3d,
                     in_=scr_d[k * 32 + half * 16:
                               k * 32 + (half + 1) * 16].rearrange(
                         "c p q -> p c q")),
                 reads=["scr_d"], writes=[pk],
                 war=[pk] + [("asm", j) for j in range(4)],
                 sem=(DQ_GA if pk == ("pnlA",) else DQ_GB), inc=16)
            for ocol in (2 * half, 2 * half + 1):
                for tb in range(TT):
                    hkeys = [("hdqT", 0, tb // 4), ("hdqT", 1, tb // 4)]
                    p.group(
                        "tensor",
                        [lambda e, ic=ic, tb=tb, ocol=ocol, pnl3d=pnl3d:
                         e.matmul(
                            psh(tb), hdqT_3d[:, ic, tb * 128:(tb + 1) * 128],
                            pnl3d[:, (ocol % 2) * 8 + ic, :],
                            start=(ic == 0), stop=(ic == 7))
                         for ic in range(8)],
                        reads=hkeys + [pk],
                        writes=[("ps", tb)], war=[("ps", tb)])
                    sl = slice(tb * DM + ocol * 512,
                               tb * DM + (ocol + 1) * 512)
                    p.op("vector",
                         lambda e, tb=tb, sl=sl: e.scalar_tensor_tensor(
                             x_res[:, sl], psh(tb), 0.5, x_res[:, sl],
                             OP.mult, OP.add),
                         reads=[("ps", tb), ("x", tb)], writes=[("x", tb)])
                    if ocol == 3:
                        if bg is not None:
                            next(bg, None)
                        if store:
                            y3 = y_out.rearrange("(q t) f -> q t f", t=TT)
                            p.op("scalar",
                                 lambda e, tb=tb, y3=y3: e.dma_start(
                                     out=y3[:, tb, :],
                                     in_=x_res[:, tb * DM:(tb + 1) * DM]),
                                 reads=[("x", tb)], writes=[("y", tb)],
                                 sem=DQ_X, inc=16)

    xnT_3d = xdqT_3d

    def ln2_gen():
        for tb in range(TT):
            layernorm_tile(tb)
            p.op("vector", lambda e: e.tensor_copy(wdqb[:, :], xnb[:, :]),
                 reads=["xnb"], writes=[("wdq",)])
            p.op("sync",
                 lambda e, tb=tb: e.dma_start(
                     out=xdqT_3d[:, :, tb * 128:(tb + 1) * 128],
                     in_=wdqb[:, :].rearrange("p (c q) -> p c q", q=128),
                     transpose=True),
                 reads=[("wdq",)], writes=[("xdqT", tb)], sem=DQ_T, inc=16)
            yield

    def attention(bg=None):
        xnT_keys = [("xdqT", tb) for tb in range(TT)]

        for wsrc, bias, bkey, name in ((wk, bk_s, "bk_s", "k"),
                                       (wv, bv_s, "bv_s", "v"),
                                       (wq, bq_s, "bq_s", "q")):
            for oc in range(16):
                ci = oc % 2
                p.op("scalar",
                     lambda e, wsrc=wsrc, oc=oc, ci=ci: e.dma_start(
                         out=wcol[ci], in_=wsrc[oc]),
                     writes=[("wcol", ci)], war=[("wcol", ci)],
                     sem=(DQ_GC if ci else DQ_G), inc=16)
                p.op("scalar",
                     lambda e, ci=ci: e.copy(wcolb[ci], wcol[ci]),
                     reads=[("wcol", ci)], writes=[("wcolb", ci)])
                for tg in range(2):
                    jps = (oc * 2 + tg) % 8
                    mm = [lambda e, fc=fc, tg=tg, ci=ci, jps=jps: e.matmul(
                        psh(jps),
                        wcolb[ci][:, fc * 128:(fc + 1) * 128],
                        xnT_3d[:, fc, tg * 512:(tg + 1) * 512],
                        start=(fc == 0), stop=(fc == FC - 1))
                        for fc in range(FC)]
                    p.group("tensor", mm,
                            reads=xnT_keys + [("wcolb", ci)],
                            writes=[("ps", jps)], war=[("ps", jps)])
                    if name == "q":
                        p.op("vector",
                             lambda e, oc=oc, tg=tg, jps=jps, bias=bias:
                             e.tensor_scalar(
                                 qt_buf[:, oc * T + tg * 512:
                                        oc * T + (tg + 1) * 512],
                                 psh(jps), bias[:, oc:oc + 1],
                                 None, OP.add),
                             reads=[("ps", jps), bkey], writes=[("qt", oc)])
                    elif name == "k":
                        p.op("vector",
                             lambda e, tg=tg, jps=jps, bias=bias, oc=oc:
                             e.tensor_scalar(
                                 kstage[:, tg * 512:(tg + 1) * 512],
                                 psh(jps), bias[:, oc:oc + 1],
                                 None, OP.add),
                             reads=[("ps", jps), bkey], writes=[("kst", tg)])
                    else:
                        p.op("vector",
                             lambda e, tg=tg, jps=jps, bias=bias, oc=oc:
                             e.tensor_scalar(
                                 vstage[:, tg * 512:(tg + 1) * 512],
                                 psh(jps), bias[:, oc:oc + 1],
                                 None, OP.add),
                             reads=[("ps", jps), bkey], writes=[("vst", tg)])
                if name == "k":
                    p.op("scalar",
                         lambda e, oc=oc: e.dma_start(
                             out=kt_bounce[oc * 128:(oc + 1) * 128, :],
                             in_=kstage),
                         reads=[("kst", 0), ("kst", 1)], writes=[("ktb", oc)],
                         sem=DQ_GKB, inc=16)
                elif name == "v":
                    p.op("sync",
                         lambda e: e.dma_start(
                             out=vtr.rearrange("p (j q) -> p j q", q=128),
                             in_=vstage.rearrange("p (j q) -> p j q", q=128),
                             transpose=True),
                         reads=[("vst", 0), ("vst", 1)], writes=["vtr"],
                         war=["vtr"], sem=DQ_T, inc=16)
                    p.op("scalar",
                         lambda e, oc=oc: e.dma_start(
                             out=v_bounce[:, oc * 128:(oc + 1) * 128]
                             .rearrange("(j p) q -> p j q", p=128),
                             in_=vtr.rearrange("p (j q) -> p j q", q=128)),
                         reads=["vtr"], writes=[("vb", oc)], sem=DQ_GVB, inc=16)
            groups = [[0, 1, 2, 3], [4, 5, 6, 7]]
            if name == "k":
                p.op("gpsimd",
                     lambda e: e.collective_compute(
                         "AllGather", OP.bypass, replica_groups=groups,
                         ins=[kt_bounce.ap().opt()], outs=[kt_all.ap().opt()]),
                     reads=[("ktb", oc) for oc in range(16)],
                     writes=["kt_all"], sem=CC, inc=1)
            elif name == "v":
                p.op("gpsimd",
                     lambda e: e.collective_compute(
                         "AllGather", OP.bypass, replica_groups=groups,
                         ins=[v_bounce.ap().opt()], outs=[v_all.ap().opt()]),
                     reads=[("vb", oc) for oc in range(16)],
                     writes=["v_all"], sem=CC, inc=1)

        p.barrier()

        def load_head(h, i):
            for r in range(RANKS):
                p.op("gpsimd",
                     lambda e, h=h, r=r, i=i: e.dma_start(
                         out=kbuf[i][:, r * T:(r + 1) * T],
                         in_=kt_all[r * DM + h * 128:
                                    r * DM + (h + 1) * 128, :]),
                     reads=["kt_all"], writes=[("kbuf", i)],
                     war=[("kbuf", i)], sem=(DQ_GK if i else DQ_G), inc=16)
                p.op("gpsimd",
                     lambda e, h=h, r=r, i=i: e.dma_start(
                         out=vbuf[i].rearrange(
                             "p (c q) -> p c q", q=128)[:, r * 8:(r + 1) * 8,
                                                        :],
                         in_=v_all[r * T:(r + 1) * T,
                                   h * 128:(h + 1) * 128].rearrange(
                             "(c p) q -> p c q", p=128)),
                     reads=["v_all"], writes=[("vbuf", i)],
                     war=[("vbuf", i)], sem=(DQ_GV if i else DQ_GVB), inc=16)

        load_head(0, 0)
        NCH = S // 128
        NB2 = NCH // 2
        psA_keys = [("ps", j) for j in range(4)]
        psO_keys = [("ps", 4), ("ps", 5)]
        psS_keys = [("ps", 6), ("ps", 7)]
        for h in range(NH):
            i = h % 2
            if h + 1 < NH:
                load_head(h + 1, (h + 1) % 2)

            def av_batch(bc):
                c0, c1 = 2 * bc, 2 * bc + 1
                mm = []
                for cj, c in enumerate((c0, c1)):
                    for tg in range(2):
                        mm.append(lambda e, c=c, cj=cj, tg=tg, i=i, bc=bc:
                                  e.matmul(
                                      psO[:, tg * 512:(tg + 1) * 512],
                                      vbuf[i][:, c * 128:(c + 1) * 128],
                                      pt[bc % 3][:, cj * 1024 + tg * 512:
                                                 cj * 1024 + (tg + 1) * 512],
                                      start=(c == 0), stop=(c == NCH - 1)))
                p.group("tensor", mm,
                        reads=[("pt", bc % 3), ("vbuf", i)],
                        writes=psO_keys if c1 == NCH - 1 else [],
                        war=psO_keys if bc == 0 else [])

            def sum_batch(bc):
                c0, c1 = 2 * bc, 2 * bc + 1
                mm = []
                for cj, c in enumerate((c0, c1)):
                    for tg in range(2):
                        mm.append(lambda e, c=c, cj=cj, tg=tg, bc=bc:
                                  e.matmul(
                                      psS[0:1, tg * 512:(tg + 1) * 512],
                                      ones_bf[:, 0:1],
                                      pt[bc % 3][:, cj * 1024 + tg * 512:
                                                 cj * 1024 + (tg + 1) * 512],
                                      start=(c == 0), stop=(c == NCH - 1)))
                p.group("tensor", mm,
                        reads=[("pt", bc % 3)],
                        writes=psS_keys if c1 == NCH - 1 else [],
                        war=psS_keys if bc == 0 else [])

            for bc in range(NB2):
                c0, c1 = 2 * bc, 2 * bc + 1
                mm = []
                for cj, c in enumerate((c0, c1)):
                    for tg in range(2):
                        mm.append(lambda e, c=c, tg=tg, cj=cj, i=i, h=h:
                                  e.matmul(
                                      psA[:, cj * 1024 + tg * 512:
                                          cj * 1024 + (tg + 1) * 512],
                                      kbuf[i][:, c * 128:(c + 1) * 128],
                                      qt_buf[:, h * T + tg * 512:
                                             h * T + (tg + 1) * 512],
                                      start=True, stop=True))
                p.group("tensor", mm, reads=[("kbuf", i), ("qt", h)],
                        writes=psA_keys, war=psA_keys)
                p.op("scalar",
                     lambda e, bc=bc: e.activation(
                         pt[bc % 3][:, :], psA[:, :], AF.Exp,
                         scale=SCALE_QK),
                     reads=psA_keys, writes=[("pt", bc % 3)])
                if bc >= 2:
                    sum_batch(bc - 2)
                if bc >= 1:
                    av_batch(bc - 1)
            av_batch(NB2 - 1)
            sum_batch(NB2 - 2)
            sum_batch(NB2 - 1)
            p.op("scalar", lambda e: e.copy(otstage, psO[:, :]),
                 reads=psO_keys, writes=["otb"], war=["otb"])
            p.op("vector", lambda e: e.tensor_copy(serow[0:1, :],
                                                   psS[0:1, :]),
                 reads=psS_keys, writes=["serow"], war=["serow"])
            p.op("vector", lambda e: e.reciprocal(serec[0:1, :],
                                                  serow[0:1, :]),
                 reads=["serow"], writes=["serec"])
            p.group("tensor",
                    [lambda e, tg=tg: e.matmul(
                        psS[:, tg * 512:(tg + 1) * 512],
                        onesf[0:1, 0:128],
                        serec[0:1, tg * 512:(tg + 1) * 512],
                        start=True, stop=True) for tg in range(2)],
                    reads=["serec", "onesrow"], writes=psS_keys,
                    war=psS_keys)
            p.op("scalar", lambda e: e.copy(serbs, psS[:, :]),
                 reads=psS_keys, writes=["serbs"])
            p.op("vector",
                 lambda e: e.tensor_tensor(otstage, otstage, serbs,
                                           op=OP.mult),
                 reads=["otb", "serbs"], writes=["otb"])
            p.op("scalar",
                 lambda e, h=h: e.dma_start(out=ot_scr[h], in_=otstage),
                 reads=["otb"], writes=[("ot", h)], sem=DQ_S, inc=16)

        p.barrier()

        for ocol in range(4):
            for dc in range(16):
                di = dc % 2
                p.op("gpsimd",
                     lambda e, dc=dc, di=di: e.dma_start(
                         out=otrow[di], in_=ot_scr[dc]),
                     reads=[("ot", dc)], writes=[("otrow", di)],
                     war=[("otrow", di)], sem=(DQ_GO if di else DQ_G), inc=16)
                p.op("gpsimd",
                     lambda e, dc=dc, ocol=ocol, di=di: e.dma_start(
                         out=wrow[di],
                         in_=wo[dc][:, ocol * 512:(ocol + 1) * 512]),
                     writes=[("wrow", di)], war=[("wrow", di)],
                     sem=(DQ_GR if di else DQ_GKB), inc=16)
                p.op("vector",
                     lambda e, di=di: e.tensor_copy(wrowb[di], wrow[di]),
                     reads=[("wrow", di)], writes=[("wrowb", di)])
                for tb in range(TT):
                    p.group(
                        "tensor",
                        [lambda e, dc=dc, tb=tb, di=di: e.matmul(
                            psh(tb),
                            otrow[di][:, tb * 128:(tb + 1) * 128],
                            wrowb[di], start=(dc == 0), stop=False)],
                        reads=[("otrow", di), ("wrowb", di)],
                        writes=[],
                        war=[("ps", tb)] if dc == 0 else [],
                    )
            for tb in range(TT):
                p.group(
                    "tensor",
                    [lambda e, tb=tb, ocol=ocol: e.matmul(
                        psh(tb), ones_bf[0:1, 0:128],
                        bo_bf[0:1, ocol * 512:(ocol + 1) * 512],
                        start=False, stop=True)],
                    reads=["ones_bf", "borow"],
                    writes=[("ps", tb)],
                )
            for tb in range(TT):
                sl = slice(tb * DM + ocol * 512, tb * DM + (ocol + 1) * 512)
                p.op("vector",
                     lambda e, tb=tb, sl=sl: e.scalar_tensor_tensor(
                         x_res[:, sl], psh(tb), 1.0, x_res[:, sl],
                         OP.mult, OP.add),
                     reads=[("ps", tb), ("x", tb)], writes=[("x", tb)])
                if bg is not None and ocol == 3:
                    next(bg, None)

    _bg = ln_and_quant_x_gen()
    quant_my_weights(bg=_bg)
    for _ in _bg:
        pass
    _ln2 = ln2_gen()
    for k in range(KS):
        ffn_chunk(k, bg=_ln2 if k == KS - 1 else None)
    for _ in _ln2:
        pass
    p.barrier()
    _bg2 = ln_and_quant_x_gen()
    attention(bg=_bg2)
    for _ in _bg2:
        pass
    p.barrier()
    for k in range(KS):
        ffn_chunk(k, store=(k == KS - 1))
    p.barrier()

    p.emit(nc, es)
    es.close()
    return nc


def kernel(L0_attn_x, L0_attn_Wq, L0_attn_bq, L0_attn_Wk, L0_attn_bk,
           L0_attn_Wv, L0_attn_bv, L0_attn_Wout, L0_attn_bout,
           L0_ffn_gate_w, L0_ffn_up_w, L0_ffn_down_w):
    x = np.ascontiguousarray(
        np.asarray(L0_attn_x, dtype=np.float32).reshape(2 * S, DM))
    gw = np.asarray(L0_ffn_gate_w, np.float32)
    uw = np.asarray(L0_ffn_up_w, np.float32)
    dwf = np.asarray(L0_ffn_down_w, np.float32)

    def blk(W):
        W = np.asarray(W, dtype=np.float32)
        return np.ascontiguousarray(
            W.reshape(16, 128, 16, 128).transpose(0, 3, 2, 1)
        ).reshape(16, 128, DM)

    def col(b):
        return np.ascontiguousarray(
            np.asarray(b, dtype=np.float32).reshape(16, 128).T)

    common = {
        "wq": blk(L0_attn_Wq), "wk": blk(L0_attn_Wk), "wv": blk(L0_attn_Wv),
        "wo": np.ascontiguousarray(
            np.asarray(L0_attn_Wout, np.float32).reshape(DM, 16, 128)
            .transpose(1, 2, 0)),
        "bqc": col(L0_attn_bq), "bkc": col(L0_attn_bk),
        "bvc": col(L0_attn_bv),
        "bo": np.ascontiguousarray(
            np.asarray(L0_attn_bout, np.float32).reshape(1, DM)),
    }
    in_maps = []
    for c in range(NCORES):
        m = dict(common)
        m["x"] = np.ascontiguousarray(x[c * T:(c + 1) * T])
        m["gw_my"] = np.ascontiguousarray(gw[c * KI:(c + 1) * KI])
        m["uw_my"] = np.ascontiguousarray(uw[c * KI:(c + 1) * KI])
        m["dw_my"] = np.ascontiguousarray(dwf[:, c * KI:(c + 1) * KI])
        in_maps.append(m)

    nc = build()
    res = run_bass_kernel_spmd(nc, in_maps, core_ids=list(range(NCORES)))
    y = np.concatenate([res.results[c]["y"] for c in range(NCORES)], axis=0)
    return y.reshape(2, S, DM)
